# version 04 (final)
# baseline (speedup 1.0000x reference)
from contextlib import ExitStack
import numpy as np
import concourse.bass as bass
import concourse.mybir as mybir

F32 = mybir.dt.float32
BF16 = mybir.dt.bfloat16
AF = mybir.ActivationFunctionType
ALU = mybir.AluOpType

D = 4096
T = 1024
NC = D // 128
NH = 32


def host_col(b):
    return np.ascontiguousarray(b.reshape(-1, 128).T)


def build_qkv():
    nc = bass.Bass("TRN2", target_bir_lowering=False)
    x0t = nc.dram_tensor("x0t", [D, T], F32, kind="ExternalInput").ap()
    wT = {n: nc.dram_tensor(n, [D, D], F32, kind="ExternalInput").ap()
          for n in ("wqT", "wkT", "wvT")}
    bq_c = nc.dram_tensor("bq_c", [128, 32], F32, kind="ExternalInput").ap()
    bk_c = nc.dram_tensor("bk_c", [128, 32], F32, kind="ExternalInput").ap()
    bv_row = nc.dram_tensor("bv_row", [1, D], F32, kind="ExternalInput").ap()
    q_o = nc.dram_tensor("q", [D, T], BF16, kind="ExternalOutput").ap()
    k_o = nc.dram_tensor("k", [D, T], BF16, kind="ExternalOutput").ap()
    v_o = nc.dram_tensor("v", [T, D], BF16, kind="ExternalOutput").ap()
    xb_o = nc.dram_tensor("xb", [D, T], BF16, kind="ExternalOutput").ap()
    mqmk_o = nc.dram_tensor("mqmk", [1, 64], F32, kind="ExternalOutput").ap()

    jobs = [("q", i) for i in range(NC)] + [("k", i) for i in range(NC)] \
        + [("v", i) for i in range(8)]
    WBLK = NC * 512

    with ExitStack() as _st:
        en = _st.enter_context
        x_sb = en(nc.sbuf_tensor("x_sb", [128, NC * T], BF16))
        w_sb = en(nc.sbuf_tensor("w_sb", [128, 2 * WBLK], BF16))
        bqk_sb = en(nc.sbuf_tensor("bqk_sb", [128, 64], F32))
        bv_sb = en(nc.sbuf_tensor("bv_sb", [1, D], BF16))
        ones_sb = en(nc.sbuf_tensor("ones_sb", [1, 128], BF16))
        onescol = en(nc.sbuf_tensor("onescol", [128, 1], BF16))
        out_sb = en(nc.sbuf_tensor("out_sb", [128, 2 * T], BF16))
        sq_sb = en(nc.sbuf_tensor("sq_sb", [128, T], BF16))
        nrm_sb = en(nc.sbuf_tensor("nrm_sb", [1, 64], F32))
        trash = en(nc.sbuf_tensor("trash", [1, T], F32))
        ps0 = en(nc.psum_tensor("ps0", [128, 512], F32))
        ps1 = en(nc.psum_tensor("ps1", [128, 512], F32))
        psn = en(nc.psum_tensor("psn", [1, T], F32))
        W0 = en(nc.semaphore("W0"))
        W1 = en(nc.semaphore("W1"))
        Wb = [W0, W1]
        M = en(nc.semaphore("M"))
        E = en(nc.semaphore("E"))
        NM = en(nc.semaphore("NM"))
        VS = en(nc.semaphore("VS"))
        SQ = en(nc.semaphore("SQ"))
        ST0 = en(nc.semaphore("ST0"))
        ST1 = en(nc.semaphore("ST1"))
        STp = [ST0, ST1]
        XLS = en(nc.semaphore("XLS"))
        XLG = en(nc.semaphore("XLG"))
        block = en(nc.Block())
        ps = [ps0, ps1]
        ntiles = {}
        for j, (kind, i) in enumerate(jobs):
            ntiles[j] = 2 if kind in ("q", "k") else 8
        tile_base = [0]
        for j in range(len(jobs)):
            tile_base.append(tile_base[-1] + ntiles[j])
        NT = tile_base[-1]

        @block.sync
        def _(sync):
            sync.dma_start(out=bqk_sb[:, 0:32], in_=bq_c[:, :]).then_inc(XLS, 16)
            sync.dma_start(out=bqk_sb[:, 32:64], in_=bk_c[:, :]).then_inc(XLS, 16)
            st = 0
            for j, (kind, i) in enumerate(jobs):
                for tt in range(ntiles[j]):
                    gt = tile_base[j] + tt
                    sync.wait_ge(E, gt + 1)
                    src = out_sb[:, (gt % 2) * T:(gt % 2) * T + 512]
                    if kind == "q":
                        sync.dma_start(out=q_o[i * 128:(i + 1) * 128, tt * 512:(tt + 1) * 512], in_=src)\
                            .then_inc(STp[gt % 2], 16)
                    elif kind == "k":
                        sync.dma_start(out=k_o[i * 128:(i + 1) * 128, tt * 512:(tt + 1) * 512], in_=src)\
                            .then_inc(STp[gt % 2], 16)
                    else:
                        sync.dma_start(out=v_o[tt * 128:(tt + 1) * 128, i * 512:(i + 1) * 512], in_=src)\
                            .then_inc(STp[gt % 2], 16)
                    st += 16
            sync.wait_ge(ST0, 16 * ((NT + 1) // 2))
            sync.wait_ge(ST1, 16 * (NT // 2))
            sync.wait_ge(VS, 64)
            sync.dma_start(out=mqmk_o[:, :], in_=nrm_sb[:, :]).then_inc(ST0, 16)
            sync.wait_ge(ST0, 16 * ((NT + 1) // 2) + 16)
            sync.wait_ge(XLG, 16 * 3)

        @block.gpsimd
        def _(g):
            g.dma_start(out=x_sb[:, :].rearrange("p (a t) -> p a t", a=NC),
                        in_=x0t.rearrange("(a p) t -> p a t", p=128)).then_inc(XLG, 16)
            g.memset(ones_sb[:, :], 1.0)
            g.memset(onescol[:, :], 1.0)
            g.dma_start(out=bv_sb[:, :], in_=bv_row[:, :]).then_inc(XLG, 16)
            g.wait_ge(XLG, 16 * 2)
            g.dma_start(out=xb_o.rearrange("(a p) t -> p a t", p=128),
                        in_=x_sb[:, :].rearrange("p (a t) -> p a t", a=NC)).then_inc(XLG, 16)
            for j, (kind, i) in enumerate(jobs):
                buf = j % 2
                if j >= 2:
                    g.wait_ge(E, tile_base[j - 1])
                if kind in ("q", "k"):
                    src = wT["wqT" if kind == "q" else "wkT"]
                    src = src.rearrange("(a p) o -> p a o", p=128)[:, :, i * 128:(i + 1) * 128]
                    dst = w_sb[:, buf * WBLK:buf * WBLK + NC * 128].rearrange("p (a m) -> p a m", a=NC)
                else:
                    src = wT["wvT"].rearrange("(a p) o -> p a o", p=128)[:, :, i * 512:(i + 1) * 512]
                    dst = w_sb[:, buf * WBLK:buf * WBLK + NC * 512].rearrange("p (a m) -> p a m", a=NC)
                g.dma_start(out=dst, in_=src).then_inc(Wb[buf], 16)

        @block.tensor
        def _(t):
            t.wait_ge(XLG, 16 * 2)
            nmm = 0
            for j, (kind, i) in enumerate(jobs):
                buf = j % 2
                t.wait_ge(Wb[buf], (j // 2 + 1) * 16)
                wb = w_sb[:, buf * WBLK:]
                for tt in range(ntiles[j]):
                    gt = tile_base[j] + tt
                    pst = ps[gt % 2]
                    if gt >= 2:
                        t.wait_ge(E, gt - 1)
                    last = None
                    if kind in ("q", "k"):
                        for a in range(NC):
                            last = t.matmul(pst[:, :], wb[:, a * 128:(a + 1) * 128],
                                            x_sb[:, a * T + tt * 512:a * T + (tt + 1) * 512],
                                            start=(a == 0), stop=(a == NC - 1))
                    else:
                        t.matmul(pst[:, :], ones_sb[0:1, :], bv_sb[0:1, i * 512:(i + 1) * 512],
                                 start=True, stop=False)
                        for a in range(NC):
                            last = t.matmul(pst[:, :], x_sb[:, a * T + tt * 128:a * T + tt * 128 + 128],
                                            wb[:, a * 512:(a + 1) * 512],
                                            start=False, stop=(a == NC - 1))
                    last.then_inc(M, 1)
                if kind in ("q", "k"):
                    t.wait_ge(SQ, j + 1)
                    if j >= 1:
                        t.wait_ge(VS, j)
                    t.matmul(psn[0:1, 0:512], onescol[:, 0:1], sq_sb[:, 0:512], start=True, stop=True)\
                        .then_inc(NM, 1)
                    t.matmul(psn[0:1, 512:1024], onescol[:, 0:1], sq_sb[:, 512:1024], start=True, stop=True)\
                        .then_inc(NM, 1)
                    nmm += 1

        @block.scalar
        def _(s):
            s.wait_ge(XLS, 32)
            for j, (kind, i) in enumerate(jobs):
                for tt in range(ntiles[j]):
                    gt = tile_base[j] + tt
                    pst = ps[gt % 2]
                    s.wait_ge(M, gt + 1)
                    if gt >= 2:
                        s.wait_ge(STp[gt % 2], (gt // 2) * 16)
                    dst = out_sb[:, (gt % 2) * T:(gt % 2) * T + 512]
                    if kind == "q":
                        s.activation(out=dst, in_=pst[:, :], func=AF.Identity,
                                     bias=bqk_sb[:, i:i + 1], scale=1.0).then_inc(E, 1)
                    elif kind == "k":
                        s.activation(out=dst, in_=pst[:, :], func=AF.Identity,
                                     bias=bqk_sb[:, 32 + i:32 + i + 1], scale=1.0).then_inc(E, 1)
                    else:
                        s.activation(out=dst, in_=pst[:, :], func=AF.Copy, bias=0.0, scale=1.0)\
                            .then_inc(E, 1)
                if kind in ("q", "k"):
                    if j >= 1:
                        s.wait_ge(NM, 2 * j)
                    b0 = (tile_base[j] + 0) % 2
                    b1 = (tile_base[j] + 1) % 2
                    s.activation(out=sq_sb[:, 0:512], in_=out_sb[:, b0 * T:b0 * T + 512],
                                 func=AF.Square, bias=0.0, scale=1.0)
                    s.activation(out=sq_sb[:, 512:1024], in_=out_sb[:, b1 * T:b1 * T + 512],
                                 func=AF.Square, bias=0.0, scale=1.0).then_inc(SQ, 1)

        @block.vector
        def _(v):
            nj = 0
            for j, (kind, i) in enumerate(jobs):
                if kind not in ("q", "k"):
                    continue
                nj += 1
                v.wait_ge(NM, 2 * nj)
                idx = i if kind == "q" else 32 + i
                v.tensor_scalar(out=trash[0:1, :], in0=psn[0:1, :], scalar1=0.0, scalar2=-1e30,
                                op0=ALU.add, op1=ALU.max, accum_out=nrm_sb[0:1, idx:idx + 1])\
                    .then_inc(VS, 1)

    return nc


def build_attn():
    nc = bass.Bass("TRN2", target_bir_lowering=False)
    S = 4096
    q_i = nc.dram_tensor("q", [D, T], BF16, kind="ExternalInput").ap()
    kf = nc.dram_tensor("kf", [D, S], BF16, kind="ExternalInput").ap()
    vf = nc.dram_tensor("vf", [S, D], BF16, kind="ExternalInput").ap()
    mq_i = nc.dram_tensor("mq", [1, 32], F32, kind="ExternalInput").ap()
    mka = nc.dram_tensor("mkall", [1, 128], F32, kind="ExternalInput").ap()
    c_o = nc.dram_tensor("c", [D, T], BF16, kind="ExternalOutput").ap()

    NKC = S // 128
    with ExitStack() as _st:
        en = _st.enter_context
        k_sb = en(nc.sbuf_tensor("k_sb", [128, 2 * S], BF16))
        v_sb = en(nc.sbuf_tensor("v_sb", [128, 2 * S], BF16))
        q_sb = en(nc.sbuf_tensor("q_sb", [128, 2 * T], BF16))
        pb_sb = en(nc.sbuf_tensor("pb_sb", [128, 2 * 512], BF16))
        c_sb = en(nc.sbuf_tensor("c_sb", [128, NC * T], BF16))
        mq_sb = en(nc.sbuf_tensor("mq_sb", [1, 32], F32))
        mk_sb = en(nc.sbuf_tensor("mk_sb", [1, 128], F32))
        t0_sb = en(nc.sbuf_tensor("t0_sb", [1, 32], F32))
        cb_sb = en(nc.sbuf_tensor("cb_sb", [128, 32], F32))
        rd_sb = en(nc.sbuf_tensor("rd_sb", [1, 512], F32))
        rb_sb = en(nc.sbuf_tensor("rb_sb", [128, 512], F32))
        onescol = en(nc.sbuf_tensor("onescol", [128, 1], BF16))
        onesf = en(nc.sbuf_tensor("onesf", [1, 128], F32))
        pss0 = en(nc.psum_tensor("pss0", [128, 512], F32))
        pss1 = en(nc.psum_tensor("pss1", [128, 512], F32))
        ps_c = en(nc.psum_tensor("ps_c", [128, 512], F32))
        ps_d = en(nc.psum_tensor("ps_d", [1, 512], F32))
        ps_b = en(nc.psum_tensor("ps_b", [128, 512], F32))
        KL0 = en(nc.semaphore("KL0"))
        KL1 = en(nc.semaphore("KL1"))
        MQL = en(nc.semaphore("MQL"))
        CBR = en(nc.semaphore("CBR"))
        SMM = en(nc.semaphore("SMM"))
        EX = en(nc.semaphore("EX"))
        PVM = en(nc.semaphore("PVM"))
        RC = en(nc.semaphore("RC"))
        BC = en(nc.semaphore("BC"))
        RB = en(nc.semaphore("RB"))
        CE = en(nc.semaphore("CE"))
        STO = en(nc.semaphore("STO"))
        GIN = en(nc.semaphore("GIN"))
        block = en(nc.Block())
        KLp = [KL0, KL1]
        SCALE = 1.0 / np.sqrt(128.0)

        def load_head(eng, h):
            buf = h % 2
            eng.dma_start(out=k_sb[:, buf * S:(buf + 1) * S],
                          in_=kf[h * 128:(h + 1) * 128, :]).then_inc(KLp[buf], 16)
            eng.dma_start(out=v_sb[:, buf * S:(buf + 1) * S].rearrange("p (a d) -> p a d", a=NKC),
                          in_=vf.rearrange("(a p) d -> p a d", p=128)[:, :, h * 128:(h + 1) * 128])\
                .then_inc(KLp[buf], 16)
            eng.dma_start(out=q_sb[:, buf * T:(buf + 1) * T],
                          in_=q_i[h * 128:(h + 1) * 128, :]).then_inc(KLp[buf], 16)

        @block.sync
        def _(sync):
            sync.dma_start(out=mq_sb[:, :], in_=mq_i[:, :]).then_inc(MQL, 16)
            sync.dma_start(out=mk_sb[:, :], in_=mka[:, :]).then_inc(MQL, 16)
            load_head(sync, 0)
            load_head(sync, 1)
            for h in range(2, NH):
                sync.wait_ge(CE, 2 * (h - 1))
                load_head(sync, h)
            sync.wait_ge(CE, 64)
            sync.dma_start(out=c_o.rearrange("(a p) t -> p a t", p=128),
                           in_=c_sb[:, :].rearrange("p (a t) -> p a t", a=NC)).then_inc(STO, 16)
            sync.wait_ge(STO, 16)

        @block.gpsimd
        def _(g):
            g.memset(onescol[:, :], 1.0)
            g.memset(onesf[:, :], 1.0).then_inc(GIN, 1)

        @block.vector
        def _(v):
            v.wait_ge(MQL, 32)
            v.tensor_tensor(out=mk_sb[0:1, 0:32], in0=mk_sb[0:1, 0:32], in1=mk_sb[0:1, 32:64], op=ALU.max)
            v.tensor_tensor(out=mk_sb[0:1, 0:32], in0=mk_sb[0:1, 0:32], in1=mk_sb[0:1, 64:96], op=ALU.max)
            v.tensor_tensor(out=mk_sb[0:1, 0:32], in0=mk_sb[0:1, 0:32], in1=mk_sb[0:1, 96:128], op=ALU.max)
            v.tensor_tensor(out=t0_sb[0:1, :], in0=mq_sb[0:1, :], in1=mk_sb[0:1, 0:32], op=ALU.mult)\
                .then_inc(CBR, 1)
            v.wait_ge(CBR, 2)
            v.tensor_scalar(out=t0_sb[0:1, :], in0=t0_sb[0:1, :], scalar1=-SCALE, scalar2=None,
                            op0=ALU.mult).then_inc(CBR, 1)

        @block.scalar
        def _(s):
            s.wait_ge(CBR, 1)
            s.activation(out=t0_sb[0:1, :], in_=t0_sb[0:1, :], func=AF.Ln, bias=0.0, scale=1.0)
            s.activation(out=t0_sb[0:1, :], in_=t0_sb[0:1, :], func=AF.Exp, bias=0.0, scale=0.5)\
                .then_inc(CBR, 1)
            s.wait_ge(CBR, 5)
            it = 0
            for h in range(NH):
                for tq in range(2):
                    for tkc in range(NKC):
                        pss = [pss0, pss1][it % 2]
                        s.wait_ge(SMM, it + 1)
                        if it >= 2:
                            s.wait_ge(PVM, it - 1)
                        s.activation(out=pb_sb[:, (it % 2) * 512:(it % 2 + 1) * 512],
                                     in_=pss[:, :], func=AF.Exp,
                                     bias=cb_sb[:, h:h + 1], scale=SCALE).then_inc(EX, 1)
                        it += 1

        @block.tensor
        def _(t):
            t.wait_ge(GIN, 1)
            t.wait_ge(CBR, 3)
            t.matmul(ps_b[:, 0:32], onesf[0:1, :], t0_sb[0:1, :], start=True, stop=True)\
                .then_inc(CBR, 1)
            t.wait_ge(CBR, 5)
            it = 0
            for h in range(NH):
                buf = h % 2
                t.wait_ge(KLp[buf], 48 * (h // 2 + 1))
                for tq in range(2):
                    htq = h * 2 + tq
                    base = htq * NKC

                    def scores(tkc, it0):
                        pss = [pss0, pss1][it0 % 2]
                        if it0 >= 2:
                            t.wait_ge(EX, it0 - 1)
                        t.matmul(pss[:, :], k_sb[:, buf * S + tkc * 128:buf * S + (tkc + 1) * 128],
                                 q_sb[:, buf * T + tq * 512:buf * T + (tq + 1) * 512],
                                 start=True, stop=True).then_inc(SMM, 1)

                    def denpv(tkc):
                        jt = base + tkc
                        t.wait_ge(EX, jt + 1)
                        if tkc == 0:
                            t.wait_ge(RC, htq)
                            t.wait_ge(CE, htq)
                        pb = pb_sb[:, (jt % 2) * 512:(jt % 2 + 1) * 512]
                        t.matmul(ps_d[0:1, :], onescol[:, 0:1], pb,
                                 start=(tkc == 0), stop=(tkc == NKC - 1))
                        t.matmul(ps_c[:, :], v_sb[:, buf * S + tkc * 128:buf * S + (tkc + 1) * 128],
                                 pb, start=(tkc == 0), stop=(tkc == NKC - 1)).then_inc(PVM, 1)

                    scores(0, it); it += 1
                    scores(1, it); it += 1
                    for tkc in range(2, NKC):
                        denpv(tkc - 2)
                        scores(tkc, it); it += 1
                    denpv(NKC - 2)
                    denpv(NKC - 1)
                    t.wait_ge(RC, htq + 1)
                    if htq >= 1:
                        t.wait_ge(RB, htq)
                    t.matmul(ps_b[:, :], onesf[0:1, :], rd_sb[0:1, :], start=True, stop=True)\
                        .then_inc(BC, 1)

        @block.vector
        def _(v):
            v.wait_ge(CBR, 4)
            v.tensor_copy(cb_sb[:, :], ps_b[:, 0:32]).then_inc(CBR, 1)
            for h in range(NH):
                for tq in range(2):
                    htq = h * 2 + tq
                    v.wait_ge(PVM, (htq + 1) * NKC)
                    v.reciprocal(rd_sb[0:1, :], ps_d[0:1, :]).then_inc(RC, 1)
                    v.wait_ge(BC, htq + 1)
                    v.tensor_copy(rb_sb[:, :], ps_b[:, :]).then_inc(RB, 1)
                    v.tensor_tensor(out=c_sb[:, h * T + tq * 512:h * T + (tq + 1) * 512],
                                    in0=ps_c[:, :], in1=rb_sb[:, :], op=ALU.mult).then_inc(CE, 1)

    return nc


def build_addln(b_is_bf16=True, b_tm=False):
    nc = bass.Bass("TRN2", target_bir_lowering=False)
    at = nc.dram_tensor("at", [D, T], F32, kind="ExternalInput").ap()
    if b_tm:
        bt = nc.dram_tensor("bt", [T, D], BF16, kind="ExternalInput").ap()
    else:
        bt = nc.dram_tensor("bt", [D, T], BF16 if b_is_bf16 else F32, kind="ExternalInput").ap()
    xf_o = nc.dram_tensor("xf", [D, T], F32, kind="ExternalOutput").ap()
    xb_o = nc.dram_tensor("xb", [D, T], BF16, kind="ExternalOutput").ap()

    with ExitStack() as _st:
        en = _st.enter_context
        xr_sb = en(nc.sbuf_tensor("xr_sb", [128, NC * T], F32))
        a_in = en(nc.sbuf_tensor("a_in", [128, 2 * T], F32))
        b_in = en(nc.sbuf_tensor("b_in", [128, 2 * T], BF16 if b_is_bf16 else F32))
        xb_t = en(nc.sbuf_tensor("xb_t", [128, 2 * T], BF16))
        sq_t = en(nc.sbuf_tensor("sq_t", [128, 2 * T], BF16))
        onescol = en(nc.sbuf_tensor("onescol", [128, 1], BF16))
        onesf = en(nc.sbuf_tensor("onesf", [1, 128], F32))
        epsc = en(nc.sbuf_tensor("epsc", [1, 1], F32))
        row = en(nc.sbuf_tensor("row", [1, 5 * T], F32))
        abc_sb = en(nc.sbuf_tensor("abc_sb", [128, 2 * T], F32))
        out_f = en(nc.sbuf_tensor("out_f", [128, 2 * T], F32))
        out_b = en(nc.sbuf_tensor("out_b", [128, 2 * T], BF16))
        ps_sum = en(nc.psum_tensor("ps_sum", [1, T], F32))
        ps_sq = en(nc.psum_tensor("ps_sq", [1, T], F32))
        ps_bc = en(nc.psum_tensor("ps_bc", [128, T], F32))
        LD0 = en(nc.semaphore("LD0"))
        LD1 = en(nc.semaphore("LD1"))
        ADD = en(nc.semaphore("ADD"))
        SQS = en(nc.semaphore("SQS"))
        MMS = en(nc.semaphore("MMS"))
        ROW = en(nc.semaphore("ROW"))
        BCM = en(nc.semaphore("BCM"))
        NRM = en(nc.semaphore("NRM"))
        STO0 = en(nc.semaphore("STO0"))
        STO1 = en(nc.semaphore("STO1"))
        GIN = en(nc.semaphore("GIN"))
        LDp = [LD0, LD1]
        STp = [STO0, STO1]
        block = en(nc.Block())

        @block.gpsimd
        def _(g):
            g.memset(onescol[:, :], 1.0)
            g.memset(epsc[:, :], 1e-5)
            g.memset(onesf[:, :], 1.0).then_inc(GIN, 1)

        @block.sync
        def _(sync):
            for c in range(NC):
                if c >= 2:
                    sync.wait_ge(ADD, c - 1)
                sync.dma_start(out=a_in[:, (c % 2) * T:(c % 2 + 1) * T],
                               in_=at[c * 128:(c + 1) * 128, :]).then_inc(LDp[c % 2], 16)
                if b_tm:
                    sync.dma_start(out=b_in[:, (c % 2) * T:(c % 2 + 1) * T],
                                   in_=bt[:, c * 128:(c + 1) * 128],
                                   transpose=True).then_inc(LDp[c % 2], 16)
                else:
                    sync.dma_start(out=b_in[:, (c % 2) * T:(c % 2 + 1) * T],
                                   in_=bt[c * 128:(c + 1) * 128, :]).then_inc(LDp[c % 2], 16)
            for c in range(NC):
                sync.wait_ge(NRM, c + 1)
                sync.dma_start(out=xf_o[c * 128:(c + 1) * 128, :],
                               in_=out_f[:, (c % 2) * T:(c % 2 + 1) * T]).then_inc(STp[c % 2], 16)
                sync.dma_start(out=xb_o[c * 128:(c + 1) * 128, :],
                               in_=out_b[:, (c % 2) * T:(c % 2 + 1) * T]).then_inc(STp[c % 2], 16)
            sync.wait_ge(STO0, 32 * 16)
            sync.wait_ge(STO1, 32 * 16)

        @block.vector
        def _(v):
            for c in range(NC):
                v.wait_ge(LDp[c % 2], 32 * (c // 2 + 1))
                if c >= 2:
                    v.wait_ge(MMS, c - 1)
                v.tensor_tensor(out=xr_sb[:, c * T:(c + 1) * T],
                                in0=a_in[:, (c % 2) * T:(c % 2 + 1) * T],
                                in1=b_in[:, (c % 2) * T:(c % 2 + 1) * T], op=ALU.add)
                v.tensor_copy(xb_t[:, (c % 2) * T:(c % 2 + 1) * T],
                              xr_sb[:, c * T:(c + 1) * T]).then_inc(ADD, 1)
            v.wait_ge(MMS, NC)
            v.tensor_scalar(out=row[0:1, 0:T], in0=ps_sum[0:1, :], scalar1=1.0 / D, scalar2=None,
                            op0=ALU.mult)
            v.tensor_scalar(out=row[0:1, T:2 * T], in0=ps_sq[0:1, :], scalar1=1.0 / D, scalar2=None,
                            op0=ALU.mult)
            v.tensor_tensor(out=row[0:1, 2 * T:3 * T], in0=row[0:1, 0:T], in1=row[0:1, 0:T],
                            op=ALU.mult)
            v.tensor_tensor(out=row[0:1, 2 * T:3 * T], in0=row[0:1, T:2 * T],
                            in1=row[0:1, 2 * T:3 * T], op=ALU.subtract).then_inc(ROW, 1)
            v.wait_ge(ROW, 2)
            v.tensor_tensor(out=row[0:1, 4 * T:5 * T], in0=row[0:1, 0:T], in1=row[0:1, 3 * T:4 * T],
                            op=ALU.mult)
            v.tensor_scalar(out=row[0:1, 4 * T:5 * T], in0=row[0:1, 4 * T:5 * T], scalar1=-1.0,
                            scalar2=None, op0=ALU.mult).then_inc(ROW, 1)
            v.wait_ge(BCM, 2)
            v.tensor_copy(abc_sb[:, 0:T], ps_bc[:, :]).then_inc(ROW, 1)
            v.wait_ge(BCM, 4)
            v.tensor_copy(abc_sb[:, T:2 * T], ps_bc[:, :]).then_inc(ROW, 1)
            for c in range(NC):
                if c >= 2:
                    v.wait_ge(STp[c % 2], 32 * (c // 2))
                v.tensor_tensor(out=out_f[:, (c % 2) * T:(c % 2 + 1) * T],
                                in0=xr_sb[:, c * T:(c + 1) * T], in1=abc_sb[:, 0:T], op=ALU.mult)
                v.tensor_tensor(out=out_f[:, (c % 2) * T:(c % 2 + 1) * T],
                                in0=out_f[:, (c % 2) * T:(c % 2 + 1) * T], in1=abc_sb[:, T:2 * T],
                                op=ALU.add)
                v.tensor_copy(out_b[:, (c % 2) * T:(c % 2 + 1) * T],
                              out_f[:, (c % 2) * T:(c % 2 + 1) * T]).then_inc(NRM, 1)

        @block.scalar
        def _(s):
            for c in range(NC):
                s.wait_ge(ADD, c + 1)
                s.activation(out=sq_t[:, (c % 2) * T:(c % 2 + 1) * T],
                             in_=xb_t[:, (c % 2) * T:(c % 2 + 1) * T],
                             func=AF.Square, bias=0.0, scale=1.0).then_inc(SQS, 1)
            s.wait_ge(ROW, 1)
            s.wait_ge(GIN, 1)
            s.activation(out=row[0:1, 3 * T:4 * T], in_=row[0:1, 2 * T:3 * T], func=AF.Ln,
                         bias=epsc[0:1, 0:1], scale=1.0)
            s.activation(out=row[0:1, 3 * T:4 * T], in_=row[0:1, 3 * T:4 * T], func=AF.Exp,
                         bias=0.0, scale=-0.5).then_inc(ROW, 1)

        @block.tensor
        def _(t):
            t.wait_ge(GIN, 1)
            for c in range(NC):
                t.wait_ge(SQS, c + 1)
                for tt in range(2):
                    t.matmul(ps_sum[0:1, tt * 512:(tt + 1) * 512], onescol[:, 0:1],
                             xb_t[:, (c % 2) * T + tt * 512:(c % 2) * T + (tt + 1) * 512],
                             start=(c == 0), stop=(c == NC - 1))
                    last = t.matmul(ps_sq[0:1, tt * 512:(tt + 1) * 512], onescol[:, 0:1],
                                    sq_t[:, (c % 2) * T + tt * 512:(c % 2) * T + (tt + 1) * 512],
                                    start=(c == 0), stop=(c == NC - 1))
                last.then_inc(MMS, 1)
            t.wait_ge(ROW, 2)
            t.matmul(ps_bc[:, 0:512], onesf[0:1, :], row[0:1, 3 * T:3 * T + 512],
                     start=True, stop=True).then_inc(BCM, 1)
            t.matmul(ps_bc[:, 512:1024], onesf[0:1, :], row[0:1, 3 * T + 512:4 * T],
                     start=True, stop=True).then_inc(BCM, 1)
            t.wait_ge(ROW, 4)
            t.matmul(ps_bc[:, 0:512], onesf[0:1, :], row[0:1, 4 * T:4 * T + 512],
                     start=True, stop=True).then_inc(BCM, 1)
            t.matmul(ps_bc[:, 512:1024], onesf[0:1, :], row[0:1, 4 * T + 512:5 * T],
                     start=True, stop=True).then_inc(BCM, 1)

    return nc

RMS_EPS = 1.1920929e-07
N_ITER = 50
TH = T + 3
PROJ = 18560
NOC = 145


"""L0 sinkhorn-FFN phase builder (dry-run scheduled)."""




def ffn_consts():
    I16p = np.zeros((80, 128), np.float32)
    I16p[64:80, 0:16] = np.eye(16, dtype=np.float32)
    SUMJ = np.zeros((16, 4), np.float32)
    SUMI = np.zeros((16, 4), np.float32)
    for i in range(4):
        for j in range(4):
            SUMJ[4 * i + j, i] = 1.0
            SUMI[4 * i + j, j] = 1.0
    BU = np.zeros((4, 128), np.float32)
    BV = np.zeros((4, 128), np.float32)
    for i in range(4):
        for j in range(4):
            BU[i, 4 * i + j] = -1.0
            BV[j, 4 * i + j] = -1.0
    E4 = np.zeros((4, 4 * 128), np.float32)
    for n in range(4):
        E4[n, n * 128:(n + 1) * 128] = 1.0
    E4p = np.zeros((36, 4 * 128), np.float32)
    for n in range(4):
        E4p[32 + n, n * 128:(n + 1) * 128] = 1.0
    E16 = np.zeros((16, 16 * 128), np.float32)
    for r in range(16):
        E16[r, r * 128:(r + 1) * 128] = 1.0
    diag01 = np.zeros((16, 1), np.float32)
    for i in range(4):
        diag01[5 * i, 0] = 1.0
    c36 = np.zeros((36, 1), np.float32)
    c36[32:36] = 1.0
    sum4c = np.zeros((4, 128), np.float32)
    sum4c[:, 0] = 1.0
    sum36c = np.zeros((36, 128), np.float32)
    sum36c[32:36, 0] = 1.0
    ones128 = np.ones((128, 1), np.float32)
    ones128b = np.ones((128, 1), np.float32)
    onesrow = np.ones((1, 128), np.float32)
    epsrms = np.full((1, 1), RMS_EPS, np.float32)
    return dict(I16p=I16p, SUMJ=SUMJ, SUMI=SUMI, BU=BU, BV=BV, E4=E4, E4p=E4p,
                E16=E16, diag01=diag01, c36=c36, sum4c=sum4c, sum36c=sum36c, ones128=ones128, ones128b=ones128b,
                onesrow=onesrow, epsrms=epsrms)


class _Dry:
    def __getattr__(self, name):
        def f(*a, **k):
            return self
        return f

    def then_inc(self, *a, **k):
        return self


class Chain:
    OPS = ("tensor_tensor", "tensor_scalar", "tensor_copy", "reciprocal",
           "activation", "tensor_add", "tensor_sub", "tensor_mul",
           "tensor_scalar_mul", "tensor_scalar_add", "tensor_tensor_scan",
           "scalar_tensor_tensor", "transpose", "memset", "iota")

    def __init__(self, eng, sem, cnt, key, state):
        self._eng = eng
        self._sem = sem
        self._cnt = cnt
        self._key = key
        self._state = state

    def __getattr__(self, name):
        f = getattr(self._eng, name)
        if name not in Chain.OPS:
            return f

        def g(*a, **k):
            if self._state["emit"]:
                self._eng.wait_ge(self._sem, self._cnt.get(self._key, 0))
            r = f(*a, **k)
            self._cnt[self._key] = self._cnt.get(self._key, 0) + 1
            if self._state["emit"]:
                r.then_inc(self._sem, 1)
            return r
        return g


def build_ffn():
    nc = bass.Bass("TRN2", target_bir_lowering=False)
    x1f = nc.dram_tensor("x1f", [D, T], F32, kind="ExternalInput").ap()
    x1b = nc.dram_tensor("x1b", [D, T], BF16, kind="ExternalInput").ap()
    wcatT = nc.dram_tensor("wcatT", [D, 96], F32, kind="ExternalInput").ap()
    basec = nc.dram_tensor("basec", [96, 1], F32, kind="ExternalInput").ap()
    gate = nc.dram_tensor("gate", [1, 1], F32, kind="ExternalInput").ap()
    wbranch = nc.dram_tensor("wbranch", [1024, 1024], F32, kind="ExternalInput").ap()
    CONST = {k: nc.dram_tensor(k, list(v.shape), F32, kind="ExternalInput").ap()
             for k, v in ffn_consts().items()}
    xr2 = nc.dram_tensor("xr2", [D, T], F32, kind="ExternalOutput").ap()

    cnt = {}
    mark = {}

    with ExitStack() as _st:
        en = _st.enter_context
        xb_sb = en(nc.sbuf_tensor("xb_sb", [128, NC * T], BF16))
        wcat_sb = en(nc.sbuf_tensor("wcat_sb", [128, 32 * 96], BF16))
        wbr_sb = en(nc.sbuf_tensor("wbr_sb", [128, 8 * 8 * 128], BF16))
        basec_sb = en(nc.sbuf_tensor("basec_sb", [96, 1], F32))
        sq_t = en(nc.sbuf_tensor("sq_t", [128, 2 * T], BF16))
        logit_sb = en(nc.sbuf_tensor("logit_sb", [96, T], F32))
        row_sb = en(nc.sbuf_tensor("row_sb", [1, 4 * T], F32))
        e16_sb = en(nc.sbuf_tensor("e16_sb", [16, T], BF16))
        lnu_sb = en(nc.sbuf_tensor("lnu_sb", [4, T], F32))
        lnv_sb = en(nc.sbuf_tensor("lnv_sb", [4, T], F32))
        hresp_sb = en(nc.sbuf_tensor("hresp_sb", [16, T], F32))
        hres_sb = en(nc.sbuf_tensor("hres_sb", [16, T], BF16))
        epre_sb = en(nc.sbuf_tensor("epre_sb", [4, T], F32))
        ppre_sb = en(nc.sbuf_tensor("ppre_sb", [4, T], BF16))
        epp_sb = en(nc.sbuf_tensor("epp_sb", [36, T], F32))
        ppp_sb = en(nc.sbuf_tensor("ppp_sb", [36, T], BF16))
        g_sb = en(nc.sbuf_tensor("g_sb", [1, 2], F32))
        gcol_sb = en(nc.sbuf_tensor("gcol_sb", [16, 1], F32))
        dvec_sb = en(nc.sbuf_tensor("dvec_sb", [16, 1], F32))
        bcb_sb = en(nc.sbuf_tensor("bcb_sb", [128, 5 * T], BF16))
        xpre_sb = en(nc.sbuf_tensor("xpre_sb", [128, 8 * T], BF16))
        y_sb = en(nc.sbuf_tensor("y_sb", [128, 8 * T], BF16))
        x1f_t = en(nc.sbuf_tensor("x1f_t", [128, 2 * T], F32))
        acc_t = en(nc.sbuf_tensor("acc_t", [128, 2 * T], F32))
        tmp_t = en(nc.sbuf_tensor("tmp_t", [128, T], BF16))
        CS = {k: en(nc.sbuf_tensor(k + "_c", list(v.shape),
                                   BF16 if k in ("SUMJ", "SUMI", "E4", "E4p", "E16", "ones128b") else F32))
              for k, v in ffn_consts().items()}
        NBF = 5
        P1 = en(nc.psum_tensor("P1", [128, T], F32))
        P2 = en(nc.psum_tensor("P2", [128, T], F32))
        P3 = en(nc.psum_tensor("P3", [128, T], F32))
        P4 = en(nc.psum_tensor("P4", [128, T], F32))
        SEM = {}
        for s_name in ["SL", "GLX", "GLW", "GLB", "RMA", "LMM", "PM",
                       "CHD", "ACA", "LXF0", "LXF1", "STO0", "STO1", "GIN"]:
            SEM[s_name] = en(nc.semaphore(s_name))
        block = en(nc.Block())
        BF_CONSTS = ("SUMJ", "SUMI", "E4", "E4p", "E16", "ones128b")
        LXp = [SEM["LXF0"], SEM["LXF1"]]
        STp = [SEM["STO0"], SEM["STO1"]]
        NCONST = len(ffn_consts())

        state = {"emit": False}

        def W(eng, s_name, val):
            if state["emit"]:
                eng.wait_ge(SEM[s_name], val)

        def Wm(eng, s_name, key):
            if state["emit"]:
                eng.wait_ge(SEM[s_name], mark[key])

        def inc(instr, s_name, n=1):
            cnt[s_name] = cnt.get(s_name, 0) + n
            if state["emit"]:
                instr.then_inc(SEM[s_name], n)
            return cnt[s_name]

        def setmark(key, s_name):
            mark[key] = cnt.get(s_name, 0)



        def act_body(s):
            W(s, "GLX", 16)
            for c in range(NC):
                if c >= 2:
                    W(s, "RMA", c - 1)
                s.activation(out=sq_t[:, (c % 2) * T:(c % 2 + 1) * T],
                             in_=xb_sb[:, c * T:(c + 1) * T], func=AF.Square,
                             bias=0.0, scale=1.0)
                setmark(f"sq_{c}", "ACA")
            Wm(s, "CHD", "msq_done")
            W(s, "SL", 16 * (NCONST + 1 - 6))
            s.activation(out=row_sb[0:1, T:2 * T], in_=row_sb[0:1, 0:T], func=AF.Ln,
                         bias=CS["epsrms"][0:1, 0:1], scale=1.0)
            s.activation(out=row_sb[0:1, T:2 * T], in_=row_sb[0:1, T:2 * T], func=AF.Exp,
                         bias=0.0, scale=-0.5)
            setmark("rstd", "ACA")
            Wm(s, "CHD", "z_ready")
            s.activation(out=epre_sb[0:4, :], in_=logit_sb[0:4, :], func=AF.Exp,
                         bias=0.0, scale=1.0)
            s.activation(out=epp_sb[32:36, :], in_=logit_sb[32:36, :], func=AF.Exp,
                         bias=0.0, scale=1.0)
            setmark("ppexp", "ACA")
            for k in range(2 * N_ITER):
                Wm(s, "PM", f"sk_build_{k}")
                s.activation(out=e16_sb[:, :], in_=P3[0:16, :], func=AF.Exp,
                             bias=0.0, scale=1.0)
                setmark(f"sk_exp_{k}", "ACA")
                Wm(s, "PM", f"sk_s4_{k}")
                dst = lnu_sb if (k % 2 == 0) else lnv_sb
                s.activation(out=dst[0:4, :], in_=P4[0:4, :], func=AF.Ln,
                             bias=0.0, scale=1.0)
                setmark(f"sk_ln_{k}", "ACA")
            Wm(s, "PM", "sk_final")
            s.activation(out=hresp_sb[:, :], in_=P3[0:16, :], func=AF.Exp,
                         bias=0.0, scale=1.0)
            setmark("hres_raw", "ACA")
            W(s, "GLB", 16 * 8)
            s.activation(out=g_sb[0:1, 0:1], in_=g_sb[0:1, 0:1], func=AF.Sigmoid,
                         bias=0.0, scale=1.0)
            setmark("sig", "ACA")
            for g_i in range(16):
                Wm(s, "PM", f"y_{g_i}")
                co, tt = divmod(g_i, 2)
                s.activation(out=y_sb[:, co * T + tt * 512:co * T + (tt + 1) * 512],
                             in_=P1[:, (g_i % 2) * 512:(g_i % 2 + 1) * 512],
                             func=AF.Copy, bias=0.0, scale=1.0)
                setmark(f"yev_{g_i}", "ACA")

        def pe_body(t):
            W(t, "GLX", 16)
            W(t, "SL", 16 * (NCONST + 1 - 6))
            W(t, "GLB", 16 * 8)
            for c in range(NC):
                Wm(t, "ACA", f"sq_{c}")
                t.matmul(P4[0:1, 0:512], CS["ones128b"][:, 0:1],
                         sq_t[:, (c % 2) * T:(c % 2) * T + 512],
                         start=(c == 0), stop=(c == NC - 1))
                inc(t.matmul(P4[0:1, 512:1024], CS["ones128b"][:, 0:1],
                             sq_t[:, (c % 2) * T + 512:(c % 2) * T + 1024],
                             start=(c == 0), stop=(c == NC - 1)), "RMA")
            W(t, "GLW", 16)
            for c in range(NC):
                for tt in range(2):
                    i = t.matmul(P1[0:96, tt * 512:(tt + 1) * 512], wcat_sb[:, c * 96:(c + 1) * 96],
                                 xb_sb[:, c * T + tt * 512:c * T + (tt + 1) * 512],
                                 start=(c == 0), stop=(c == NC - 1))
                    if c == NC - 1 and tt == 1:
                        inc(i, "LMM")
            Wm(t, "ACA", "rstd")
            t.matmul(P2[0:96, 0:512], CS["onesrow"][0:1, 0:96], row_sb[0:1, T:T + 512],
                     start=True, stop=True)
            setmark("b96", "PM")
            inc(t.matmul(P2[0:96, 512:1024], CS["onesrow"][0:1, 0:96], row_sb[0:1, T + 512:2 * T],
                         start=True, stop=True), "PM")
            mark["b96"] = cnt["PM"]
            Wm(t, "ACA", "ppexp")
            Wm(t, "CHD", "msq_done")
            t.matmul(P4[:, 0:512], CS["sum4c"][0:4, :], epre_sb[0:4, 0:512],
                     start=True, stop=True)
            t.matmul(P4[:, 512:1024], CS["sum4c"][0:4, :], epre_sb[0:4, 512:1024],
                     start=True, stop=True)
            t.matmul(P3[:, 0:512], CS["sum36c"][:, :], epp_sb[:, 0:512], start=True, stop=True)
            inc(t.matmul(P3[:, 512:1024], CS["sum36c"][:, :], epp_sb[:, 512:1024],
                         start=True, stop=True), "PM")
            mark["ppsum"] = cnt["PM"]
            Wm(t, "CHD", "recips")
            t.matmul(P2[:, 0:512], CS["onesrow"][0:1, :], row_sb[0:1, 2 * T:2 * T + 512],
                     start=True, stop=True)
            inc(t.matmul(P2[:, 512:1024], CS["onesrow"][0:1, :], row_sb[0:1, 2 * T + 512:3 * T],
                         start=True, stop=True), "PM")
            mark["bcpre"] = cnt["PM"]
            Wm(t, "CHD", "ppre_done")
            t.matmul(P2[:, 0:512], CS["onesrow"][0:1, :], row_sb[0:1, 3 * T:3 * T + 512],
                     start=True, stop=True)
            inc(t.matmul(P2[:, 512:1024], CS["onesrow"][0:1, :],
                         row_sb[0:1, 3 * T + 512:4 * T], start=True, stop=True), "PM")
            mark["bcpost"] = cnt["PM"]
            for n in range(4):
                if n == 0:
                    Wm(t, "CHD", "ppp_done")
                else:
                    Wm(t, "CHD", f"bc4ev_{n - 1}")
                t.matmul(P2[:, 0:512], CS["E4"][0:4, n * 128:(n + 1) * 128],
                         ppre_sb[0:4, 0:512], start=True, stop=True)
                inc(t.matmul(P2[:, 512:1024], CS["E4"][0:4, n * 128:(n + 1) * 128],
                             ppre_sb[0:4, 512:1024], start=True, stop=True), "PM")
                mark[f"bc4_{n}"] = cnt["PM"]
            for k in range(2 * N_ITER):
                if k == 0:
                    Wm(t, "CHD", "z_ready")
                else:
                    Wm(t, "ACA", f"sk_ln_{k - 1}")
                i = None
                for tt in range(2):
                    sl = slice(tt * 512, (tt + 1) * 512)
                    if k == 0:
                        i = t.matmul(P3[:, sl], CS["I16p"][:, :], logit_sb[0:80, sl],
                                     start=True, stop=True)
                    else:
                        t.matmul(P3[:, sl], CS["I16p"][:, :], logit_sb[0:80, sl],
                                 start=True, stop=False)
                        src = lnv_sb if (k % 2 == 0) else lnu_sb
                        B = CS["BV"] if (k % 2 == 0) else CS["BU"]
                        i = t.matmul(P3[:, sl], B[0:4, :], src[0:4, sl],
                                     start=False, stop=True)
                inc(i, "PM")
                mark[f"sk_build_{k}"] = cnt["PM"]
                if k == 0:
                    Wm(t, "CHD", "recips")
                Wm(t, "ACA", f"sk_exp_{k}")
                SM = CS["SUMJ"] if (k % 2 == 0) else CS["SUMI"]
                t.matmul(P4[0:4, 0:512], SM[0:16, :], e16_sb[0:16, 0:512], start=True, stop=True)
                inc(t.matmul(P4[0:4, 512:1024], SM[0:16, :], e16_sb[0:16, 512:1024],
                             start=True, stop=True), "PM")
                mark[f"sk_s4_{k}"] = cnt["PM"]
            Wm(t, "ACA", f"sk_ln_{2 * N_ITER - 1}")
            i = None
            for tt in range(2):
                sl = slice(tt * 512, (tt + 1) * 512)
                t.matmul(P3[:, sl], CS["I16p"][:, :], logit_sb[0:80, sl],
                         start=True, stop=False)
                t.matmul(P3[:, sl], CS["BU"][0:4, :], lnu_sb[0:4, sl], start=False, stop=False)
                i = t.matmul(P3[:, sl], CS["BV"][0:4, :], lnv_sb[0:4, sl],
                             start=False, stop=True)
            inc(i, "PM")
            mark["sk_final"] = cnt["PM"]
            Wm(t, "ACA", "sig")
            inc(t.matmul(P3[:, 0:2], CS["onesrow"][0:1, :], g_sb[0:1, 0:2],
                         start=True, stop=True), "PM")
            mark["gcol"] = cnt["PM"]
            Wm(t, "CHD", "xpre_all")
            for g_i in range(16):
                co, tt = divmod(g_i, 2)
                if g_i >= 2:
                    Wm(t, "ACA", f"yev_{g_i - 2}")
                i = None
                for cchunk in range(8):
                    i = t.matmul(P1[:, (g_i % 2) * 512:(g_i % 2 + 1) * 512],
                                 wbr_sb[:, cchunk * 1024 + co * 128:cchunk * 1024 + (co + 1) * 128],
                                 xpre_sb[:, cchunk * T + tt * 512:cchunk * T + (tt + 1) * 512],
                                 start=(cchunk == 0), stop=(cchunk == 7))
                inc(i, "PM")
                mark[f"y_{g_i}"] = cnt["PM"]
            slot = 0
            for i_s in range(4):
                for j_s in range(5):
                    ps = [P2, P1][slot % 2]
                    if slot >= 2:
                        Wm(t, "CHD", f"bchev_{slot - 2}")
                    elif slot == 1:
                        Wm(t, "ACA", "yev_15")
                    else:
                        Wm(t, "CHD", "hres_done")
                    if j_s < 4:
                        r = 4 * i_s + j_s
                        t.matmul(ps[:, 0:512], CS["E16"][0:16, r * 128:(r + 1) * 128],
                                 hres_sb[0:16, 0:512], start=True, stop=True)
                        i = t.matmul(ps[:, 512:1024], CS["E16"][0:16, r * 128:(r + 1) * 128],
                                     hres_sb[0:16, 512:1024], start=True, stop=True)
                    else:
                        t.matmul(ps[:, 0:512], CS["E4p"][:, i_s * 128:(i_s + 1) * 128],
                                 ppp_sb[:, 0:512], start=True, stop=True)
                        i = t.matmul(ps[:, 512:1024], CS["E4p"][:, i_s * 128:(i_s + 1) * 128],
                                     ppp_sb[:, 512:1024], start=True, stop=True)
                    inc(i, "PM")
                    mark[f"bch_{slot}"] = cnt["PM"]
                    slot += 1

        def dve_body(v):
            W(v, "LMM", 1)
            v.tensor_copy(logit_sb[:, :], P1[0:96, :])
            setmark("logit_ev", "CHD")
            W(v, "RMA", NC)
            v.tensor_scalar(out=row_sb[0:1, 0:T], in0=P4[0:1, :], scalar1=1.0 / D,
                            scalar2=None, op0=ALU.mult)
            setmark("msq_done", "CHD")
            Wm(v, "PM", "b96")
            (v.tensor_tensor(out=logit_sb[0:4, :], in0=logit_sb[0:4, :], in1=P2[0:4, :], op=ALU.mult))
            (v.tensor_tensor(out=logit_sb[32:36, :], in0=logit_sb[32:36, :], in1=P2[32:36, :],
                               op=ALU.mult))
            (v.tensor_tensor(out=logit_sb[64:80, :], in0=logit_sb[64:80, :], in1=P2[64:80, :],
                               op=ALU.mult))
            (v.tensor_scalar(out=logit_sb[0:4, :], in0=logit_sb[0:4, :],
                               scalar1=basec_sb[0:4, 0:1], scalar2=None, op0=ALU.add))
            (v.tensor_scalar(out=logit_sb[32:36, :], in0=logit_sb[32:36, :],
                               scalar1=basec_sb[32:36, 0:1], scalar2=None, op0=ALU.add))
            v.tensor_scalar(out=logit_sb[64:80, :], in0=logit_sb[64:80, :],
                            scalar1=basec_sb[64:80, 0:1], scalar2=10.0,
                            op0=ALU.add, op1=ALU.mult)
            setmark("z_ready", "CHD")
            Wm(v, "PM", "ppsum")
            v.reciprocal(row_sb[0:1, 2 * T:3 * T], P4[0:1, :])
            v.reciprocal(row_sb[0:1, 3 * T:4 * T], P3[0:1, :])
            setmark("recips", "CHD")
            Wm(v, "PM", "bcpre")
            v.tensor_tensor(out=ppre_sb[0:4, :], in0=epre_sb[0:4, :], in1=P2[0:4, :],
                            op=ALU.mult)
            setmark("ppre_done", "CHD")
            Wm(v, "PM", "bcpost")
            v.tensor_tensor(out=ppp_sb[32:36, :], in0=epp_sb[32:36, :], in1=P2[32:36, :],
                            op=ALU.mult)
            setmark("ppp_done", "CHD")
            for n in range(4):
                Wm(v, "PM", f"bc4_{n}")
                v.tensor_copy(bcb_sb[:, n * T:(n + 1) * T], P2[:, :])
                setmark(f"bc4ev_{n}", "CHD")
            for cc in range(8):
                (v.tensor_tensor(out=acc_t[:, 0:T], in0=xb_sb[:, cc * T:(cc + 1) * T],
                                   in1=bcb_sb[:, 0:T], op=ALU.mult))
                for n in range(1, 4):
                    (v.tensor_tensor(out=tmp_t[:, :],
                                       in0=xb_sb[:, (n * 8 + cc) * T:(n * 8 + cc + 1) * T],
                                       in1=bcb_sb[:, n * T:(n + 1) * T], op=ALU.mult))
                    (v.tensor_tensor(out=acc_t[:, 0:T], in0=acc_t[:, 0:T], in1=tmp_t[:, :],
                                       op=ALU.add))
                v.tensor_copy(xpre_sb[:, cc * T:(cc + 1) * T], acc_t[:, 0:T])
                setmark(f"xpd_{cc}", "CHD")
            Wm(v, "PM", "gcol")
            setmark("xpre_all", "CHD")
            v.tensor_copy(gcol_sb[:, :], P3[0:16, 0:1])
            (v.tensor_scalar(out=dvec_sb[:, :], in0=gcol_sb[:, :], scalar1=-1.0, scalar2=1.0,
                               op0=ALU.mult, op1=ALU.add))
            (v.tensor_tensor(out=dvec_sb[:, :], in0=dvec_sb[:, :], in1=CS["diag01"][:, :],
                               op=ALU.mult))
            v.tensor_scalar(out=hres_sb[:, :], in0=hresp_sb[:, :], scalar1=gcol_sb[0:16, 0:1],
                            scalar2=dvec_sb[0:16, 0:1], op0=ALU.mult, op1=ALU.add)
            setmark("hres_done", "CHD")
            slot = 0
            for i_s in range(4):
                for j_s in range(5):
                    Wm(v, "PM", f"bch_{slot}")
                    ps = [P2, P1][slot % 2]
                    v.tensor_copy(bcb_sb[:, j_s * T:(j_s + 1) * T], ps[:, :])
                    setmark(f"bchev_{slot}", "CHD")
                    slot += 1
                for cc in range(8):
                    c = i_s * 8 + cc
                    par = c % 2
                    W(v, LXp[par], 16 * (c // 2 + 1))
                    if c >= 2:
                        W(v, STp[par], 16 * (c // 2))
                    (v.tensor_tensor(out=acc_t[:, par * T:(par + 1) * T],
                                       in0=xb_sb[:, cc * T:(cc + 1) * T],
                                       in1=bcb_sb[:, 0:T], op=ALU.mult))
                    for j_s in range(1, 4):
                        (v.tensor_tensor(out=tmp_t[:, :],
                                           in0=xb_sb[:, (j_s * 8 + cc) * T:(j_s * 8 + cc + 1) * T],
                                           in1=bcb_sb[:, j_s * T:(j_s + 1) * T], op=ALU.mult))
                        (v.tensor_tensor(out=acc_t[:, par * T:(par + 1) * T],
                                           in0=acc_t[:, par * T:(par + 1) * T], in1=tmp_t[:, :],
                                           op=ALU.add))
                    Wm(v, "ACA", f"yev_{2 * cc + 1}")
                    (v.tensor_tensor(out=tmp_t[:, :], in0=y_sb[:, cc * T:(cc + 1) * T],
                                       in1=bcb_sb[:, 4 * T:5 * T], op=ALU.mult))
                    (v.tensor_tensor(out=acc_t[:, par * T:(par + 1) * T],
                                       in0=acc_t[:, par * T:(par + 1) * T], in1=tmp_t[:, :], op=ALU.add))
                    v.tensor_tensor(out=acc_t[:, par * T:(par + 1) * T],
                                    in0=acc_t[:, par * T:(par + 1) * T],
                                    in1=x1f_t[:, par * T:(par + 1) * T], op=ALU.add)
                    setmark(f"out_{c}", "CHD")

        _origW = W

        def W(eng, s, val):
            if state["emit"]:
                eng.wait_ge(SEM[s] if isinstance(s, str) else s, val)

        dry = _Dry()
        for _pass in range(2):
            cnt.clear()
            act_body(Chain(dry, None, cnt, "ACA", state))
            for kk in ("RMA", "LMM", "PM"):
                cnt[kk] = 0
            pe_body(dry)
            dve_body(Chain(dry, None, cnt, "CHD", state))
        cnt.clear()

        state["emit"] = True

        @block.sync
        def _(sync):
            sync.dma_start(out=basec_sb[:, :], in_=basec[:, :]).then_inc(SEM["SL"], 16)
            for k in ffn_consts():
                if k in BF_CONSTS:
                    continue
                sync.dma_start(out=CS[k][:, :], in_=CONST[k][:, :]).then_inc(SEM["SL"], 16)
            for c in range(NC):
                if c >= 2:
                    sync.wait_ge(SEM["CHD"], mark[f"out_{c - 2}"])
                    sync.dma_start(out=xr2[(c - 2) * 128:(c - 1) * 128, :],
                                   in_=acc_t[:, (c % 2) * T:(c % 2 + 1) * T]).then_inc(STp[c % 2], 16)
                sync.dma_start(out=x1f_t[:, (c % 2) * T:(c % 2 + 1) * T],
                               in_=x1f[c * 128:(c + 1) * 128, :]).then_inc(LXp[c % 2], 16)
            for c in (NC - 2, NC - 1):
                sync.wait_ge(SEM["CHD"], mark[f"out_{c}"])
                sync.dma_start(out=xr2[c * 128:(c + 1) * 128, :],
                               in_=acc_t[:, (c % 2) * T:(c % 2 + 1) * T]).then_inc(STp[c % 2], 16)
            sync.wait_ge(SEM["STO0"], 16 * 16)
            sync.wait_ge(SEM["STO1"], 16 * 16)

        @block.gpsimd
        def _(g):
            g.memset(g_sb[:, :], 0.0)
            g.memset(logit_sb[:, :], 0.0)
            g.memset(epp_sb[:, :], 0.0)
            g.memset(ppp_sb[:, :], 0.0)
            g.memset(epre_sb[:, :], 0.0).then_inc(SEM["GIN"], 1)
            g.dma_start(out=xb_sb[:, :].rearrange("p (a t) -> p a t", a=NC),
                        in_=x1b.rearrange("(a p) t -> p a t", p=128)).then_inc(SEM["GLX"], 16)
            g.dma_start(out=wcat_sb[:, :].rearrange("p (a m) -> p a m", a=NC),
                        in_=wcatT.rearrange("(a p) o -> p a o", p=128)).then_inc(SEM["GLW"], 16)
            g.dma_start(out=wbr_sb[:, :].rearrange("p (a b m) -> p a b m", a=8, b=8),
                        in_=wbranch.rearrange("(a p) (b m) -> p a b m", p=128, m=128))\
                .then_inc(SEM["GLB"], 16)
            for k in BF_CONSTS:
                g.dma_start(out=CS[k][:, :], in_=CONST[k][:, :]).then_inc(SEM["GLB"], 16)
            g.wait_ge(SEM["GIN"], 1)
            g.dma_start(out=g_sb[0:1, 0:1], in_=gate[:, :]).then_inc(SEM["GLB"], 16)

        @block.scalar
        def _(s):
            for kk in ("ACA",):
                cnt[kk] = 0
            act_body(Chain(s, SEM["ACA"], cnt, "ACA", state))

        @block.tensor
        def _(t):
            for kk in ("RMA", "LMM", "PM"):
                cnt[kk] = 0
            pe_body(t)

        @block.vector
        def _(v):
            for kk in ("CHD",):
                cnt[kk] = 0
            dve_body(Chain(v, SEM["CHD"], cnt, "CHD", state))

    return nc


"""L1 phases: proj+conv+dt (token-sharded), SSD (group-sharded), gate+outproj, binffn."""




def build_l1proj():
    nc = bass.Bass("TRN2", target_bir_lowering=False)
    x2b = nc.dram_tensor("x2b", [D, TH], BF16, kind="ExternalInput").ap()
    wT = nc.dram_tensor("inprojT", [D, PROJ], F32, kind="ExternalInput").ap()
    convw = nc.dram_tensor("convw_c", [128, 320], F32, kind="ExternalInput").ap()
    convb = nc.dram_tensor("convb_c", [128, 80], F32, kind="ExternalInput").ap()
    dtb = nc.dram_tensor("dtbias_col", [128, 1], F32, kind="ExternalInput").ap()
    alog = nc.dram_tensor("alog_col", [128, 1], F32, kind="ExternalInput").ap()
    z_o = nc.dram_tensor("z", [8192, T], BF16, kind="ExternalOutput").ap()
    hs_o = nc.dram_tensor("hs", [8192, T], BF16, kind="ExternalOutput").ap()
    bc_o = nc.dram_tensor("bc", [2048, T], BF16, kind="ExternalOutput").ap()
    dt_o = nc.dram_tensor("dt", [128, T], F32, kind="ExternalOutput").ap()
    dA_o = nc.dram_tensor("dA", [128, T], F32, kind="ExternalOutput").ap()

    cnt = {}
    mark = {}
    state = {"emit": False}

    with ExitStack() as _st:
        en = _st.enter_context
        x_sb = en(nc.sbuf_tensor("x_sb", [128, 32 * TH], BF16))
        w_sb = en(nc.sbuf_tensor("w_sb", [128, 2 * 32 * 128], BF16))
        cw_sb = en(nc.sbuf_tensor("cw_sb", [128, 320], F32))
        cb_sb = en(nc.sbuf_tensor("cb_sb", [128, 80], F32))
        dtb_sb = en(nc.sbuf_tensor("dtb_sb", [128, 1], F32))
        nega_sb = en(nc.sbuf_tensor("nega_sb", [128, 1], F32))
        out_t = en(nc.sbuf_tensor("out_t", [128, 2 * T], BF16))
        acc_a = en(nc.sbuf_tensor("acc_a", [128, T], F32))
        acc_b = en(nc.sbuf_tensor("acc_b", [128, T], F32))
        dt_sb = en(nc.sbuf_tensor("dt_sb", [128, T], F32))
        da_sb = en(nc.sbuf_tensor("da_sb", [128, T], F32))
        PS0 = en(nc.psum_tensor("PS0", [128, 1536], F32))
        PS1 = en(nc.psum_tensor("PS1", [128, 1536], F32))
        SEM = {}
        for s_name in ["SL", "GI", "W0", "W1", "PM", "CHD", "ACA", "STO0", "STO1"]:
            SEM[s_name] = en(nc.semaphore(s_name))
        block = en(nc.Block())
        Wb = [SEM["W0"], SEM["W1"]]
        STp = [SEM["STO0"], SEM["STO1"]]
        PS = [PS0, PS1]

        def W(eng, s, val):
            if state["emit"]:
                eng.wait_ge(SEM[s] if isinstance(s, str) else s, val)

        def Wm(eng, key):
            if state["emit"]:
                s_name, val = mark[key]
                eng.wait_ge(SEM[s_name], val)

        def inc(instr, s_name, n=1):
            cnt[s_name] = cnt.get(s_name, 0) + n
            if state["emit"]:
                instr.then_inc(SEM[s_name], n)
            return cnt[s_name]

        def setmark(key, s_name):
            mark[key] = (s_name, cnt.get(s_name, 0))

        def jtype(j):
            if j < 64:
                return "z"
            if j < 144:
                return "x"
            return "dt"

        def pe_body(t):
            W(t, "SL", 16)
            for j in range(NOC):
                buf = j % 2
                W(t, Wb[buf], 16 * (j // 2 + 1))
                if j >= 2:
                    Wm(t, f"pfree_{j - 2}")
                ps = PS[buf]
                for tt, (off, n) in enumerate([(0, 512), (512, 512), (1024, 3)]):
                    for a in range(32):
                        i = t.matmul(ps[:, off:off + n] if n == 512 else ps[:, 1024:1027],
                                     w_sb[:, buf * 4096 + a * 128:(buf * 4096) + (a + 1) * 128],
                                     x_sb[:, a * TH + off:a * TH + off + n],
                                     start=(a == 0), stop=(a == 31))
                inc(i, "PM")

        def act_body(s):
            W(s, "GI", 16 * 2)
            s.activation(out=nega_sb[:, :], in_=alog_sb_alias[:, :], func=AF.Exp,
                         bias=0.0, scale=1.0)
            setmark("nega1", "ACA")
            for j in range(NOC):
                buf = j % 2
                W(s, "PM", j + 1)
                kind = jtype(j)
                ps = PS[buf]
                if kind == "z":
                    if j >= 2:
                        W(s, STp[j % 2], 16 * (j // 2))
                    s.activation(out=out_t[:, buf * T:(buf + 1) * T], in_=ps[:, 3:1027],
                                 func=AF.Silu, bias=0.0, scale=1.0)
                    setmark(f"ev_{j}", "ACA")
                    setmark(f"pfree_{j}", "ACA")
                elif kind == "x":
                    Wm(s, f"conv_{j}")
                    if j >= 2:
                        W(s, STp[j % 2], 16 * (j // 2))
                    s.activation(out=out_t[:, buf * T:(buf + 1) * T], in_=acc_b[:, :],
                                 func=AF.Silu, bias=cb_sb[:, j - 64:j - 63], scale=1.0)
                    setmark(f"ev_{j}", "ACA")
                else:
                    s.activation(out=acc_a[:, :], in_=ps[:, 3:1027], func=AF.Abs,
                                 bias=dtb_sb[:, 0:1], scale=1.0)
                    s.activation(out=acc_a[:, :], in_=acc_a[:, :], func=AF.Exp,
                                 bias=0.0, scale=-1.0)
                    s.activation(out=acc_a[:, :], in_=acc_a[:, :], func=AF.Ln,
                                 bias=1.0, scale=1.0)
                    s.activation(out=acc_b[:, :], in_=ps[:, 3:1027], func=AF.Relu,
                                 bias=dtb_sb[:, 0:1], scale=1.0)
                    setmark(f"pfree_{j}", "ACA")
                    setmark("dt_parts", "ACA")

        def dve_body(v):
            Wm(v, "nega1")
            v.tensor_scalar(out=nega_sb[:, :], in0=nega_sb[:, :], scalar1=-1.0, scalar2=None,
                            op0=ALU.mult)
            for j in range(64, 144):
                buf = j % 2
                W(v, "PM", j + 1)
                a = j - 64
                ps = PS[buf]
                v.tensor_scalar(out=acc_a[:, :], in0=ps[:, 0:1024],
                                scalar1=cw_sb[:, a * 4:a * 4 + 1], scalar2=None, op0=ALU.mult)
                v.scalar_tensor_tensor(out=acc_b[:, :], in0=ps[:, 1:1025],
                                       scalar=cw_sb[:, a * 4 + 1:a * 4 + 2], in1=acc_a[:, :],
                                       op0=ALU.mult, op1=ALU.add)
                v.scalar_tensor_tensor(out=acc_a[:, :], in0=ps[:, 2:1026],
                                       scalar=cw_sb[:, a * 4 + 2:a * 4 + 3], in1=acc_b[:, :],
                                       op0=ALU.mult, op1=ALU.add)
                v.scalar_tensor_tensor(out=acc_b[:, :], in0=ps[:, 3:1027],
                                       scalar=cw_sb[:, a * 4 + 3:a * 4 + 4], in1=acc_a[:, :],
                                       op0=ALU.mult, op1=ALU.add)
                setmark(f"conv_{j}", "CHD")
                setmark(f"pfree_{j}", "CHD")
            Wm(v, "dt_parts")
            v.tensor_tensor(out=dt_sb[:, :], in0=acc_a[:, :], in1=acc_b[:, :], op=ALU.add)
            v.tensor_scalar(out=da_sb[:, :], in0=dt_sb[:, :], scalar1=nega_sb[:, 0:1],
                            scalar2=None, op0=ALU.mult)
            setmark("dt_done", "CHD")

        alog_sb_alias = nega_sb
        dry = _Dry()
        for _ in range(2):
            cnt.clear()
            pe_body(dry)
            act_body(Chain(dry, None, cnt, "ACA", state))
            dve_body(Chain(dry, None, cnt, "CHD", state))
        cnt.clear()
        state["emit"] = True

        @block.sync
        def _(sync):
            sync.dma_start(out=cw_sb[:, :], in_=convw[:, :]).then_inc(SEM["SL"], 16)
            sync.dma_start(out=cb_sb[:, :], in_=convb[:, :]).then_inc(SEM["SL"], 16)
            sync.dma_start(out=dtb_sb[:, :], in_=dtb[:, :]).then_inc(SEM["SL"], 16)
            sync.dma_start(out=nega_sb[:, :], in_=alog[:, :]).then_inc(SEM["SL"], 16)
            for j in range(NOC):
                kind = jtype(j)
                if kind == "dt":
                    continue
                Wm(sync, f"ev_{j}")
                src = out_t[:, (j % 2) * T:(j % 2 + 1) * T]
                if kind == "z":
                    sync.dma_start(out=z_o[j * 128:(j + 1) * 128, :], in_=src)\
                        .then_inc(STp[j % 2], 16)
                else:
                    a = j - 64
                    if a < 64:
                        sync.dma_start(out=hs_o[a * 128:(a + 1) * 128, :], in_=src)\
                            .then_inc(STp[j % 2], 16)
                    else:
                        sync.dma_start(out=bc_o[(a - 64) * 128:(a - 63) * 128, :], in_=src)\
                            .then_inc(STp[j % 2], 16)
            Wm(sync, "dt_done")
            sync.dma_start(out=dt_o[:, :], in_=dt_sb[:, :]).then_inc(STp[0], 16)
            sync.dma_start(out=dA_o[:, :], in_=da_sb[:, :]).then_inc(STp[1], 16)
            sync.wait_ge(SEM["STO0"], 16 * (72 + 1))
            sync.wait_ge(SEM["STO1"], 16 * (72 + 1))

        @block.gpsimd
        def _(g):
            g.dma_start(out=x_sb[:, :].rearrange("p (a t) -> p a t", a=32),
                        in_=x2b.rearrange("(a p) t -> p a t", p=128)).then_inc(SEM["GI"], 16)
            g.wait_ge(SEM["SL"], 16 * 4)
            g.nop().then_inc(SEM["GI"], 16)
            for j in range(NOC):
                buf = j % 2
                if j >= 2:
                    W(g, "PM", j - 1)
                g.dma_start(out=w_sb[:, buf * 4096:(buf + 1) * 4096].rearrange("p (a m) -> p a m", a=32),
                            in_=wT.rearrange("(a p) o -> p a o", p=128)[:, :, j * 128:(j + 1) * 128])\
                    .then_inc(Wb[buf], 16)

        @block.tensor
        def _(t):
            cnt.clear()
            t.wait_ge(SEM["GI"], 16)
            pe_body(t)

        @block.scalar
        def _(s):
            for kk in ("ACA",):
                cnt[kk] = 0
            act_body(Chain(s, SEM["ACA"], cnt, "ACA", state))

        @block.vector
        def _(v):
            for kk in ("CHD",):
                cnt[kk] = 0
            dve_body(Chain(v, SEM["CHD"], cnt, "CHD", state))

    return nc


def l1proj_host_inputs(inp, x2b_full, core):
    cw = inp["L1_attn_conv_w"].reshape(80, 128, 4).transpose(1, 0, 2).reshape(128, 320)
    return {
        "x2b": x2b_full,
        "inprojT": np.ascontiguousarray(inp["L1_attn_in_proj_w"].T),
        "convw_c": np.ascontiguousarray(cw),
        "convb_c": np.ascontiguousarray(inp["L1_attn_conv_b"].reshape(80, 128).T),
        "dtbias_col": inp["L1_attn_dt_bias"].reshape(128, 1),
        "alog_col": inp["L1_attn_A_log"].reshape(128, 1),
    }


def ssd_consts():
    masklog = np.zeros((128, 512), np.float32)
    for jt in range(2):
        for j in range(128):
            for i in range(256):
                if jt * 128 + j > i:
                    masklog[j, jt * 256 + i] = -1e30
    I2b = np.zeros((128, 64), np.float32)
    for r in range(128):
        I2b[r, r % 64] = 1.0
    I128 = np.eye(128, dtype=np.float32)
    SEL16 = np.zeros((16, 16 * 128), np.float32)
    for r in range(16):
        SEL16[r, r * 128:(r + 1) * 128] = 1.0
    I16 = np.eye(16, dtype=np.float32)
    ONES16x128 = np.ones((16, 128), np.float32)
    return dict(masklog=masklog, I2b=I2b, I128=I128, SEL16=SEL16, I16=I16,
                ONES16x128=ONES16x128)


def build_ssd():
    nc = bass.Bass("TRN2", target_bir_lowering=False)
    hsg = nc.dram_tensor("hsg", [1024, 8192], BF16, kind="ExternalInput").ap()
    Bg = nc.dram_tensor("Bg", [128, 8192], BF16, kind="ExternalInput").ap()
    Cg = nc.dram_tensor("Cg", [128, 8192], BF16, kind="ExternalInput").ap()
    dtg = nc.dram_tensor("dtg", [16, 8192], F32, kind="ExternalInput").ap()
    dAg = nc.dram_tensor("dAg", [16, 8192], F32, kind="ExternalInput").ap()
    dcols = nc.dram_tensor("dcols", [128, 8], F32, kind="ExternalInput").ap()
    CONST = {k: nc.dram_tensor(k, list(v.shape), F32, kind="ExternalInput").ap()
             for k, v in ssd_consts().items()}
    y_o = nc.dram_tensor("y", [1024, 8192], BF16, kind="ExternalOutput").ap()

    cnt = {}
    mark = {}
    state = {"emit": False}
    BFC = ("masklog", "I2b", "I128")

    with ExitStack() as _st:
        en = _st.enter_context
        B_sb = en(nc.sbuf_tensor("B_sb", [128, 8192], BF16))
        C_sb = en(nc.sbuf_tensor("C_sb", [128, 8192], BF16))
        dt_sb = en(nc.sbuf_tensor("dt_sb", [16, 8192], F32))
        dA_sb = en(nc.sbuf_tensor("dA_sb", [16, 8192], F32))
        hs_t = en(nc.sbuf_tensor("hs_t", [128, 2 * 8 * 256], BF16))
        hsT_sb = en(nc.sbuf_tensor("hsT_sb", [128, 2 * 1024], BF16))
        BT_sb = en(nc.sbuf_tensor("BT_sb", [128, 256], BF16))
        Bw_sb = en(nc.sbuf_tensor("Bw_sb", [128, 256], BF16))
        cum_sb = en(nc.sbuf_tensor("cum_sb", [16, 256], F32))
        r1_sb = en(nc.sbuf_tensor("r1_sb", [16, 256], F32))
        e1_sb = en(nc.sbuf_tensor("e1_sb", [16, 256], F32))
        wdl_sb = en(nc.sbuf_tensor("wdl_sb", [16, 256], F32))
        cumrep_sb = en(nc.sbuf_tensor("cumrep_sb", [16, 512], F32))
        cdec_sb = en(nc.sbuf_tensor("cdec_sb", [16, 17], F32))
        ones16 = en(nc.sbuf_tensor("ones16", [16, 256], F32))
        trT_sb = en(nc.sbuf_tensor("trT_sb", [128, 80], F32))
        ecb_sb = en(nc.sbuf_tensor("ecb_sb", [128, 256], BF16))
        Cexp_sb = en(nc.sbuf_tensor("Cexp_sb", [128, 256], BF16))
        P_sb = en(nc.sbuf_tensor("P_sb", [128, 2 * 512], BF16))
        state_sb = en(nc.sbuf_tensor("state_sb", [128, 1024], F32))
        stateb_sb = en(nc.sbuf_tensor("stateb_sb", [128, 64], BF16))
        DI_sb = en(nc.sbuf_tensor("DI_sb", [128, 8 * 64], BF16))
        dcol_sb = en(nc.sbuf_tensor("dcol_sb", [128, 8], F32))
        y_sb = en(nc.sbuf_tensor("y_sb", [128, 2 * 8 * 256], BF16))
        CS = {k: en(nc.sbuf_tensor(k + "_c", list(v.shape),
                                   BF16 if k in BFC else F32))
              for k, v in ssd_consts().items()}
        PSst = en(nc.psum_tensor("PSst", [128, 512], F32))
        PSP0 = en(nc.psum_tensor("PSP0", [128, 512], F32))
        PSP1 = en(nc.psum_tensor("PSP1", [128, 512], F32))
        PSy = en(nc.psum_tensor("PSy", [128, 256], F32))
        PST = en(nc.psum_tensor("PST", [128, 64], F32))
        PSe = en(nc.psum_tensor("PSe", [128, 256], F32))
        PSxf = en(nc.psum_tensor("PSxf", [128, 80], F32))
        PSxb = en(nc.psum_tensor("PSxb", [128, 1024], BF16))
        SEM = {}
        for s_name in ["SL", "GI", "HS0", "HS1", "PM", "CHD", "ACA", "STO0", "STO1"]:
            SEM[s_name] = en(nc.semaphore(s_name))
        block = en(nc.Block())
        HSp = [SEM["HS0"], SEM["HS1"]]
        STp = [SEM["STO0"], SEM["STO1"]]
        PSP = [PSP0, PSP1]

        def W(eng, s, val):
            if state["emit"]:
                eng.wait_ge(SEM[s] if isinstance(s, str) else s, val)

        def Wm(eng, key):
            if state["emit"]:
                s_name, val = mark[key]
                eng.wait_ge(SEM[s_name], val)

        def inc(instr, s_name, n=1):
            cnt[s_name] = cnt.get(s_name, 0) + n
            if state["emit"]:
                instr.then_inc(SEM[s_name], n)
            return cnt[s_name]

        def setmark(key, s_name):
            mark[key] = (s_name, cnt.get(s_name, 0))

        NCHUNK = 32

        def pe_body(t):
            W(t, "GI", 16 * 5)
            W(t, "SL", 16 * (3 + len(ssd_consts()) - 3))
            for ck in range(NCHUNK):
                S0 = ck * 256
                buf = ck % 2
                W(t, HSp[buf], 16 * (ck // 2 + 1))
                Wm(t, f"prep_{ck}")
                Wm(t, f"wdl_{ck}")
                Wm(t, f"diag_{ck}")
                if ck >= 1:
                    Wm(t, f"tev_last_{ck - 1}")
                for jt in range(2):
                    t.transpose(PSxf[:, jt * 16:(jt + 1) * 16],
                                r1_sb[0:16, jt * 128:(jt + 1) * 128], CS["I16"][:, :])
                    t.transpose(PSxf[:, 32 + jt * 16:32 + (jt + 1) * 16],
                                wdl_sb[0:16, jt * 128:(jt + 1) * 128], CS["I16"][:, :])
                inc(t.matmul(PSxf[:, 64:80], CS["ONES16x128"][:, :], cdec_sb[0:16, 0:16],
                             start=True, stop=True), "PM")
                mark[f"xpall_{ck}"] = ("PM", cnt["PM"])
                tiles = [("bt", jt) for jt in range(2)] + [("hs", i2) for i2 in range(16)]
                for gi, grp in enumerate([tiles[0:8], tiles[8:16], tiles[16:18]]):
                    psb = PSxb
                    if gi == 0 and ck >= 1:
                        Wm(t, f"evC_{ck - 1}")
                    if gi == 1:
                        Wm(t, f"evA_{ck}")
                    if gi == 2:
                        Wm(t, f"evB_{ck}")
                    last = None
                    for sloti, (kind2, idx) in enumerate(grp):
                        if kind2 == "bt":
                            src_ap = B_sb[:, S0 + idx * 128:S0 + (idx + 1) * 128]
                        else:
                            a2, jt2 = divmod(idx, 2)
                            src_ap = hs_t[:, buf * 2048 + a2 * 256 + jt2 * 128:
                                          buf * 2048 + a2 * 256 + (jt2 + 1) * 128]
                        last = t.transpose(psb[:, sloti * 128:(sloti + 1) * 128],
                                           src_ap, CS["I128"][:, :])
                    inc(last, "PM")
                    mark[f"xpg_{ck}_{gi}"] = ("PM", cnt["PM"])
                if ck >= 1:
                    Wm(t, f"pmul_{ck - 1}_15")
                for jt in range(2):
                    i = t.matmul(PSst[:, jt * 256:(jt + 1) * 256],
                                 B_sb[:, S0 + jt * 128:S0 + (jt + 1) * 128],
                                 C_sb[:, S0:S0 + 256], start=True, stop=True)
                inc(i, "PM")
                mark[f"st_{ck}"] = ("PM", cnt["PM"])
                for hh in range(16):
                    a, lo = divmod(hh, 2)
                    colpos = lo * 64
                    Wm(t, f"cumrep_{ck}")
                    if ck > 0 or hh > 1:
                        mark_key = f"pexp_{ck}_{hh - 2}" if hh >= 2 else f"pexp_{ck - 1}_{14 + hh}"
                        Wm(t, mark_key)
                    t.matmul(PSP[hh % 2][:, :], CS["I128"][:, :], CS["masklog"][:, :],
                             start=True, stop=False)
                    inc(t.matmul(PSP[hh % 2][:, :], CS["SEL16"][0:16, hh * 128:(hh + 1) * 128],
                                 cumrep_sb[0:16, :], start=False, stop=True), "PM")
                    mark[f"pbuild_{ck}_{hh}"] = ("PM", cnt["PM"])
                    if ck > 0 or hh > 0:
                        pk = f"ecbexp_{ck}_{hh - 1}" if hh >= 1 else f"ecbexp_{ck - 1}_15"
                        Wm(t, pk)
                    inc(t.matmul(PSe[:, :], CS["SEL16"][0:16, hh * 128:(hh + 1) * 128],
                                 cum_sb[0:16, :], start=True, stop=True), "PM")
                    mark[f"ebc_{ck}_{hh}"] = ("PM", cnt["PM"])
                    Wm(t, f"bw_{ck}_{hh}")
                    if ck > 0 or hh > 0:
                        pk = f"supd_{ck}_{hh - 1}" if hh >= 1 else f"supd_{ck - 1}_15"
                        Wm(t, pk)
                    for jt in range(2):
                        i = t.matmul(PST[:, :], Bw_sb[:, jt * 128:(jt + 1) * 128],
                                     hsT_sb[:, buf * 1024 + jt * 512 + a * 64 + lo * 32:0] if False
                                     else hsT_sb[:, (buf * 0) + jt * 1024 + a * 128 + colpos:
                                                 jt * 1024 + a * 128 + colpos + 64],
                                     start=(jt == 0), stop=(jt == 1))
                    inc(i, "PM")
                    mark[f"tmm_{ck}_{hh}"] = ("PM", cnt["PM"])
                    Wm(t, f"pmulr_{ck}_{hh}")
                    if lo == 0 and (ck > 0 or a > 0):
                        pk = f"yev_{ck}_{a - 1}" if a >= 1 else f"yev_{ck - 1}_7"
                        Wm(t, pk)
                    for jt in range(2):
                        t.matmul(PSy[0:64, 0:256] if colpos == 0 else PSy[64:128, 0:256],
                                 hsT_sb[:, jt * 1024 + a * 128 + colpos:
                                        jt * 1024 + a * 128 + colpos + 64],
                                 P_sb[:, (hh % 2) * 512 + jt * 256:(hh % 2) * 512 + (jt + 1) * 256],
                                 start=(lo == 0 and jt == 0) if False else (jt == 0 and True),
                                 stop=False, tile_position=(0, colpos))
                    if ck % 16 != 0:
                        Wm(t, f"cexp_{ck}_{hh}")
                        Wm(t, f"scast_{ck}_{hh}")
                        t.matmul(PSy[0:64, 0:256] if colpos == 0 else PSy[64:128, 0:256],
                                 stateb_sb[:, 0:64],
                                 Cexp_sb[:, :], start=False, stop=False,
                                 tile_position=(0, colpos))
                    i = t.matmul(PSy[0:64, 0:256] if colpos == 0 else PSy[64:128, 0:256],
                                 DI_sb[lo * 64:(lo + 1) * 64, a * 64:(a + 1) * 64],
                                 hs_t[:, buf * 2048 + a * 256:buf * 2048 + (a + 1) * 256]
                                 [lo * 64:(lo + 1) * 64, :],
                                 start=False, stop=True,
                                 tile_position=(lo * 64, colpos))
                    inc(i, "PM")
                    mark[f"ymm_{ck}_{hh}"] = ("PM", cnt["PM"])

        def act_body(s):
            W(s, "SL", 16 * (3 + len(ssd_consts()) - 3))
            for ck in range(NCHUNK):
                S0 = ck * 256
                Wm(s, f"cum_{ck}")
                s.activation(out=r1_sb[:, :], in_=dt_sb[:, S0:S0 + 256], func=AF.Ln,
                             bias=0.0, scale=1.0)
                setmark(f"lndt_{ck}", "ACA")
                s.activation(out=e1_sb[:, :], in_=cum_sb[:, :], func=AF.Exp,
                             bias=cum_sb[:, 255:256], scale=-1.0)
                setmark(f"e1_{ck}", "ACA")
                s.activation(out=cdec_sb[0:16, 16:17], in_=cum_sb[:, 255:256], func=AF.Exp,
                             bias=0.0, scale=1.0)
                setmark(f"cdec_{ck}", "ACA")
                for hh in range(16):
                    Wm(s, f"pbuild_{ck}_{hh}")
                    s.activation(out=P_sb[:, (hh % 2) * 512:(hh % 2) * 512 + 256],
                                 in_=PSP[hh % 2][:, 0:256], func=AF.Exp,
                                 bias=trT_sb[:, hh:hh + 1], scale=1.0)
                    s.activation(out=P_sb[:, (hh % 2) * 512 + 256:(hh % 2) * 512 + 512],
                                 in_=PSP[hh % 2][:, 256:512], func=AF.Exp,
                                 bias=trT_sb[:, 16 + hh:16 + hh + 1], scale=1.0)
                    setmark(f"pexp_{ck}_{hh}", "ACA")
                    Wm(s, f"ebc_{ck}_{hh}")
                    s.activation(out=ecb_sb[:, :], in_=PSe[:, :], func=AF.Exp,
                                 bias=0.0, scale=1.0)
                    setmark(f"ecbexp_{ck}_{hh}", "ACA")

        def dve_body(v):
            W(v, "SL", 16 * (3 + len(ssd_consts()) - 3))
            W(v, "GI", 16 * 5)
            for a in range(8):
                v.tensor_scalar(out=DI_sb[:, a * 64:(a + 1) * 64], in0=CS["I2b"][:, :],
                                scalar1=dcol_sb[:, a:a + 1], scalar2=None, op0=ALU.mult)
            setmark("di_done", "CHD")
            for ck in range(NCHUNK):
                S0 = ck * 256
                buf = ck % 2
                if ck >= 1:
                    Wm(v, f"ebc_{ck - 1}_15")
                v.tensor_tensor_scan(out=cum_sb[:, :], data0=ones16[:, :],
                                     data1=dA_sb[:, S0:S0 + 256], initial=0.0,
                                     op0=ALU.mult, op1=ALU.add)
                setmark(f"cum_{ck}", "CHD")
                Wm(v, f"lndt_{ck}")
                v.tensor_tensor(out=r1_sb[:, :], in0=r1_sb[:, :], in1=cum_sb[:, :],
                                op=ALU.subtract)
                setmark(f"prep_{ck}", "CHD")
                Wm(v, f"e1_{ck}")
                v.tensor_tensor(out=wdl_sb[:, :], in0=e1_sb[:, :], in1=dt_sb[:, S0:S0 + 256],
                                op=ALU.mult)
                setmark(f"wdl_{ck}", "CHD")
                v.tensor_copy(cumrep_sb[:, 0:256], cum_sb[:, :])
                v.tensor_copy(cumrep_sb[:, 256:512], cum_sb[:, :])
                setmark(f"cumrep_{ck}", "CHD")
                Wm(v, f"cdec_{ck}")
                v.tensor_scalar(out=cdec_sb[0:16, 0:16], in0=CS["I16"][:, :],
                                scalar1=cdec_sb[0:16, 16:17], scalar2=None, op0=ALU.mult)
                setmark(f"diag_{ck}", "CHD")
                Wm(v, f"xpall_{ck}")
                v.tensor_copy(trT_sb[:, :], PSxf[:, :])
                setmark(f"tev_last_{ck}", "CHD")
                tiles = [("bt", jt) for jt in range(2)] + [("hs", i2) for i2 in range(16)]
                for gi, grp in enumerate([tiles[0:8], tiles[8:16], tiles[16:18]]):
                    psb = PSxb
                    Wm(v, f"xpg_{ck}_{gi}")
                    for sloti, (kind2, idx) in enumerate(grp):
                        if kind2 == "bt":
                            dst = BT_sb[:, idx * 128:(idx + 1) * 128]
                        else:
                            a2, jt2 = divmod(idx, 2)
                            dst = hsT_sb[:, jt2 * 1024 + a2 * 128:jt2 * 1024 + (a2 + 1) * 128]
                        v.tensor_copy(dst, psb[:, sloti * 128:(sloti + 1) * 128])
                    setmark(["evA", "evB", "evC"][gi] + f"_{ck}", "CHD")
                for hh in range(16):
                    a, lo = divmod(hh, 2)
                    Wm(v, f"tmm_{ck}_{hh - 1}") if hh >= 1 else None
                    v.tensor_scalar(out=Bw_sb[:, 0:128], in0=BT_sb[:, 0:128],
                                    scalar1=trT_sb[:, 32 + hh:32 + hh + 1], scalar2=None, op0=ALU.mult)
                    v.tensor_scalar(out=Bw_sb[:, 128:256], in0=BT_sb[:, 128:256],
                                    scalar1=trT_sb[:, 48 + hh:48 + hh + 1], scalar2=None,
                                    op0=ALU.mult)
                    setmark(f"bw_{ck}_{hh}", "CHD")
                    Wm(v, f"pexp_{ck}_{hh}")
                    Wm(v, f"st_{ck}")
                    v.tensor_tensor(out=P_sb[:, (hh % 2) * 512:(hh % 2 + 1) * 512],
                                    in0=P_sb[:, (hh % 2) * 512:(hh % 2 + 1) * 512],
                                    in1=PSst[:, :], op=ALU.mult)
                    setmark(f"pmul_{ck}_{hh}", "CHD")
                    setmark(f"pmulr_{ck}_{hh}", "CHD")
                    if ck % 16 != 0:
                        Wm(v, f"ecbexp_{ck}_{hh}")
                        if hh >= 1:
                            Wm(v, f"ymm_{ck}_{hh - 1}")
                        elif ck >= 1:
                            Wm(v, f"ymm_{ck - 1}_15")
                        v.tensor_tensor(out=Cexp_sb[:, :], in0=C_sb[:, S0:S0 + 256],
                                        in1=ecb_sb[:, :], op=ALU.mult)
                        setmark(f"cexp_{ck}_{hh}", "CHD")
                        v.tensor_copy(stateb_sb[:, :], state_sb[:, hh * 64:(hh + 1) * 64])
                        setmark(f"scast_{ck}_{hh}", "CHD")
                    Wm(v, f"tmm_{ck}_{hh}")
                    if ck % 16 != 0:
                        Wm(v, f"ymm_{ck}_{hh}")
                        v.tensor_scalar(out=state_sb[:, hh * 64:(hh + 1) * 64],
                                        in0=state_sb[:, hh * 64:(hh + 1) * 64],
                                        scalar1=trT_sb[:, 64 + hh:64 + hh + 1], scalar2=None,
                                        op0=ALU.mult)
                        v.tensor_tensor(out=state_sb[:, hh * 64:(hh + 1) * 64],
                                        in0=state_sb[:, hh * 64:(hh + 1) * 64],
                                        in1=PST[:, :], op=ALU.add)
                    else:
                        v.tensor_copy(state_sb[:, hh * 64:(hh + 1) * 64], PST[:, :])
                    setmark(f"supd_{ck}_{hh}", "CHD")
                    if lo == 1:
                        Wm(v, f"ymm_{ck}_{hh}")
                        if ck >= 2:
                            W(v, STp[ck % 2], 16 * (ck // 2))
                        v.tensor_copy(y_sb[:, buf * 2048 + a * 256:buf * 2048 + (a + 1) * 256],
                                      PSy[:, :])
                        setmark(f"yev_{ck}_{a}", "CHD")
                setmark(f"hst_used_{ck}", "CHD")
                setmark(f"pmul_{ck}_15r", "CHD")

        dry = _Dry()
        for _ in range(2):
            cnt.clear()
            pe_body(dry)
            act_body(Chain(dry, None, cnt, "ACA", state))
            dve_body(Chain(dry, None, cnt, "CHD", state))
        cnt.clear()
        state["emit"] = True

        @block.sync
        def _(sync):
            for k in ssd_consts():
                if k in BFC:
                    continue
                sync.dma_start(out=CS[k][:, :], in_=CONST[k][:, :]).then_inc(SEM["SL"], 16)
            sync.dma_start(out=dt_sb[:, :], in_=dtg[:, :]).then_inc(SEM["SL"], 16)
            sync.dma_start(out=dA_sb[:, :], in_=dAg[:, :]).then_inc(SEM["SL"], 16)
            sync.dma_start(out=dcol_sb[:, :], in_=dcols[:, :]).then_inc(SEM["SL"], 16)
            def store_y(ck):
                Wm(sync, f"yev_{ck}_7")
                sync.dma_start(out=y_o.rearrange("(a p) t -> p a t", p=128)[:, :, ck * 256:(ck + 1) * 256],
                               in_=y_sb[:, (ck % 2) * 2048:(ck % 2 + 1) * 2048]
                               .rearrange("p (a t) -> p a t", a=8)).then_inc(STp[ck % 2], 16)
            for ck in range(NCHUNK):
                if ck >= 2:
                    store_y(ck - 2)
                    Wm(sync, f"ymm_{ck - 2}_15")
                sync.dma_start(out=hs_t[:, (ck % 2) * 2048:(ck % 2 + 1) * 2048]
                               .rearrange("p (a t) -> p a t", a=8),
                               in_=hsg.rearrange("(a p) t -> p a t", p=128)[:, :, ck * 256:(ck + 1) * 256])\
                    .then_inc(HSp[ck % 2], 16)
            store_y(NCHUNK - 2)
            store_y(NCHUNK - 1)
            sync.wait_ge(SEM["STO0"], 16 * 16)
            sync.wait_ge(SEM["STO1"], 16 * 16)

        @block.gpsimd
        def _(g):
            g.memset(ones16[:, :], 1.0)
            g.dma_start(out=B_sb[:, :], in_=Bg[:, :]).then_inc(SEM["GI"], 16)
            g.dma_start(out=C_sb[:, :], in_=Cg[:, :]).then_inc(SEM["GI"], 16)
            for k in BFC:
                g.dma_start(out=CS[k][:, :], in_=CONST[k][:, :]).then_inc(SEM["GI"], 16)
        @block.tensor
        def _(t):
            cnt.clear()
            pe_body(t)

        @block.scalar
        def _(s):
            cnt["ACA"] = 0
            act_body(Chain(s, SEM["ACA"], cnt, "ACA", state))

        @block.vector
        def _(v):
            cnt["CHD"] = 0
            dve_body(Chain(v, SEM["CHD"], cnt, "CHD", state))

    return nc


def build_gateout():
    nc = bass.Bass("TRN2", target_bir_lowering=False)
    yg = nc.dram_tensor("yg", [8192, T], BF16, kind="ExternalInput").ap()
    z_i = nc.dram_tensor("z", [8192, T], BF16, kind="ExternalInput").ap()
    outwT = nc.dram_tensor("outwT", [8192, D], F32, kind="ExternalInput").ap()
    normw = nc.dram_tensor("normw_c", [128, 64], F32, kind="ExternalInput").ap()
    onesr = nc.dram_tensor("onesrow", [1, 128], F32, kind="ExternalInput").ap()
    eps1 = nc.dram_tensor("eps1", [1, 1], F32, kind="ExternalInput").ap()
    ao_o = nc.dram_tensor("ao", [D, T], BF16, kind="ExternalOutput").ap()

    cnt = {}
    mark = {}
    state = {"emit": False}

    with ExitStack() as _st:
        en = _st.enter_context
        g_sb = en(nc.sbuf_tensor("g_sb", [128, 64 * T], BF16))
        yz_t = en(nc.sbuf_tensor("yz_t", [128, 4 * T], BF16))
        sq_t = en(nc.sbuf_tensor("sq_t", [128, 2 * T], BF16))
        vrow = en(nc.sbuf_tensor("vrow", [1, 8 * T], F32))
        rb_sb = en(nc.sbuf_tensor("rb_sb", [128, T], F32))
        nw_sb = en(nc.sbuf_tensor("nw_sb", [128, 64], F32))
        onesr_sb = en(nc.sbuf_tensor("onesr_sb", [1, 128], F32))
        onescol = en(nc.sbuf_tensor("onescol", [128, 1], BF16))
        epsc = en(nc.sbuf_tensor("epsc", [1, 1], F32))
        w_sb = en(nc.sbuf_tensor("w2_sb", [128, 64 * 128], BF16))
        out_t = en(nc.sbuf_tensor("out_t", [128, 2 * T], BF16))
        PSm = en(nc.psum_tensor("PSm", [128, T], F32))
        PSv = en(nc.psum_tensor("PSv", [1, T], F32))
        PSb = en(nc.psum_tensor("PSb", [128, T], F32))
        SEM = {}
        for s_name in ["SL", "GI", "YZ0", "YZ1", "W0", "W1", "PM", "CHD", "ACA",
                       "STO0", "STO1"]:
            SEM[s_name] = en(nc.semaphore(s_name))
        block = en(nc.Block())
        YZp = [SEM["YZ0"], SEM["YZ1"]]
        Wb = [SEM["W0"], SEM["W1"]]
        STp = [SEM["STO0"], SEM["STO1"]]

        def W(eng, s, val):
            if state["emit"]:
                eng.wait_ge(SEM[s] if isinstance(s, str) else s, val)

        def Wm(eng, key):
            if state["emit"]:
                s_name, val = mark[key]
                eng.wait_ge(SEM[s_name], val)

        def inc(instr, s_name, n=1):
            cnt[s_name] = cnt.get(s_name, 0) + n
            if state["emit"]:
                instr.then_inc(SEM[s_name], n)
            return cnt[s_name]

        def setmark(key, s_name):
            mark[key] = (s_name, cnt.get(s_name, 0))

        def act_body(s):
            W(s, "SL", 16 * 3)
            for c in range(64):
                Wm(s, f"gmul_{c}")
                if c >= 2:
                    Wm(s, f"vac_{c - 2}")
                s.activation(out=sq_t[:, (c % 2) * T:(c % 2 + 1) * T],
                             in_=g_sb[:, c * T:(c + 1) * T], func=AF.Square, bias=0.0, scale=1.0)
                setmark(f"sq_{c}", "ACA")
            Wm(s, "vev_7")
            s.activation(out=vrow[0:1, :], in_=vrow[0:1, :], func=AF.Ln,
                         bias=epsc[0:1, 0:1], scale=1.0 / 1024.0)
            s.activation(out=vrow[0:1, :], in_=vrow[0:1, :], func=AF.Exp, bias=0.0, scale=-0.5)
            setmark("rstd", "ACA")

        def pe_body(t):
            W(t, "GI", 1)
            W(t, "SL", 16 * 3)
            for c in range(64):
                Wm(t, f"sq_{c}")
                i = None
                for tt in range(2):
                    i = t.matmul(PSv[0:1, tt * 512:(tt + 1) * 512], onescol[:, 0:1],
                                 sq_t[:, (c % 2) * T + tt * 512:(c % 2) * T + (tt + 1) * 512],
                                 start=(c % 8 == 0), stop=(c % 8 == 7))
                inc(i, "PM")
                mark[f"vac_{c}"] = ("PM", cnt["PM"])
            for grp in range(8):
                Wm(t, "rstd")
                if grp > 0:
                    Wm(t, f"rbev_{grp - 1}")
                t.matmul(PSb[:, 0:512], onesr_sb[0:1, :], vrow[0:1, grp * T:grp * T + 512],
                         start=True, stop=True)
                inc(t.matmul(PSb[:, 512:1024], onesr_sb[0:1, :],
                             vrow[0:1, grp * T + 512:(grp + 1) * T], start=True, stop=True), "PM")
                mark[f"rb_{grp}"] = ("PM", cnt["PM"])
            Wm(t, "gn_63")
            for j in range(32):
                i = None
                for half in range(2):
                    W(t, Wb[half], 16 * (j + 1))
                    if half == 0 and j >= 1:
                        Wm(t, f"oev_{j - 1}")
                    for tt in range(2):
                        for a2 in range(32):
                            a = half * 32 + a2
                            i = t.matmul(PSm[:, tt * 512:(tt + 1) * 512],
                                         w_sb[:, a * 128:(a + 1) * 128],
                                         g_sb[:, a * T + tt * 512:a * T + (tt + 1) * 512],
                                         start=(a == 0), stop=(a == 63))
                    inc(i, "PM")
                    mark[f"omm{'AB'[half]}_{j}"] = ("PM", cnt["PM"])
                mark[f"omm_{j}"] = ("PM", cnt["PM"])

        def dve_body(v):
            W(v, "SL", 16 * 3)
            for c in range(64):
                W(v, YZp[c % 2], 32 * (c // 2 + 1))
                v.tensor_tensor(out=g_sb[:, c * T:(c + 1) * T],
                                in0=yz_t[:, (c % 2) * 2 * T:(c % 2) * 2 * T + T],
                                in1=yz_t[:, (c % 2) * 2 * T + T:(c % 2) * 2 * T + 2 * T],
                                op=ALU.mult)
                setmark(f"gmul_{c}", "CHD")
                if c % 8 == 7:
                    grp = c // 8
                    Wm(v, f"vac_{c}")
                    v.tensor_copy(vrow[0:1, grp * T:(grp + 1) * T], PSv[0:1, :])
                    setmark(f"vev_{grp}", "CHD")
            for grp in range(8):
                Wm(v, f"rb_{grp}")
                v.tensor_copy(rb_sb[:, :], PSb[:, :])
                setmark(f"rbev_{grp}", "CHD")
                for cc in range(8):
                    c = grp * 8 + cc
                    v.tensor_tensor(out=g_sb[:, c * T:(c + 1) * T],
                                    in0=g_sb[:, c * T:(c + 1) * T], in1=rb_sb[:, :], op=ALU.mult)
                    v.tensor_scalar(out=g_sb[:, c * T:(c + 1) * T],
                                    in0=g_sb[:, c * T:(c + 1) * T],
                                    scalar1=nw_sb[:, c:c + 1], scalar2=None, op0=ALU.mult)
                    setmark(f"gn_{c}", "CHD")
            for j in range(32):
                Wm(v, f"omm_{j}")
                if j >= 2:
                    W(v, STp[j % 2], 16 * (j // 2))
                v.tensor_copy(out_t[:, (j % 2) * T:(j % 2 + 1) * T], PSm[:, :])
                setmark(f"oev_{j}", "CHD")

        dry = _Dry()
        for _ in range(2):
            cnt.clear()
            pe_body(dry)
            act_body(Chain(dry, None, cnt, "ACA", state))
            dve_body(Chain(dry, None, cnt, "CHD", state))
        cnt.clear()
        state["emit"] = True

        @block.sync
        def _(sync):
            sync.dma_start(out=nw_sb[:, :], in_=normw[:, :]).then_inc(SEM["SL"], 16)
            sync.dma_start(out=onesr_sb[:, :], in_=onesr[:, :]).then_inc(SEM["SL"], 16)
            sync.dma_start(out=epsc[:, :], in_=eps1[:, :]).then_inc(SEM["SL"], 16)
            for c in range(64):
                if c >= 2:
                    Wm(sync, f"sq_{c - 2}")
                sync.dma_start(out=yz_t[:, (c % 2) * 2 * T:(c % 2) * 2 * T + T],
                               in_=yg[c * 128:(c + 1) * 128, :]).then_inc(YZp[c % 2], 16)
                sync.dma_start(out=yz_t[:, (c % 2) * 2 * T + T:(c % 2) * 2 * T + 2 * T],
                               in_=z_i[c * 128:(c + 1) * 128, :]).then_inc(YZp[c % 2], 16)
            for j in range(32):
                Wm(sync, f"oev_{j}")
                sync.dma_start(out=ao_o[j * 128:(j + 1) * 128, :],
                               in_=out_t[:, (j % 2) * T:(j % 2 + 1) * T]).then_inc(STp[j % 2], 16)
            sync.wait_ge(SEM["STO0"], 16 * 16)
            sync.wait_ge(SEM["STO1"], 16 * 16)

        @block.gpsimd
        def _(g):
            g.memset(onescol[:, :], 1.0).then_inc(SEM["GI"], 1)
            for j in range(32):
                for half in range(2):
                    if j >= 1:
                        W(g, "PM", mark[f"omm{'AB'[half]}_{j - 1}"][1])
                    g.dma_start(out=w_sb[:, half * 4096:(half + 1) * 4096]
                                .rearrange("p (a m) -> p a m", a=32),
                                in_=outwT.rearrange("(a p) o -> p a o", p=128)
                                [:, half * 32:(half + 1) * 32, j * 128:(j + 1) * 128])\
                        .then_inc(Wb[half], 16)

        @block.tensor
        def _(t):
            cnt.clear()
            t.wait_ge(SEM["GI"], 1)
            pe_body(t)

        @block.scalar
        def _(s):
            cnt["ACA"] = 0
            act_body(Chain(s, SEM["ACA"], cnt, "ACA", state))

        @block.vector
        def _(v):
            cnt["CHD"] = 0
            dve_body(Chain(v, SEM["CHD"], cnt, "CHD", state))

    return nc


def build_binffn():
    nc = bass.Bass("TRN2", target_bir_lowering=False)
    x3b = nc.dram_tensor("x3b", [D, T], BF16, kind="ExternalInput").ap()
    wTT = nc.dram_tensor("wTT", [D, D], F32, kind="ExternalInput").ap()
    sp = nc.dram_tensor("sp_c", [128, 32], F32, kind="ExternalInput").ap()
    ones2 = nc.dram_tensor("ones2", [1, 2], F32, kind="ExternalInput").ap()
    bo_o = nc.dram_tensor("bo", [D, T], F32, kind="ExternalOutput").ap()

    cnt = {}
    mark = {}
    state = {"emit": False}

    with ExitStack() as _st:
        en = _st.enter_context
        x_sb = en(nc.sbuf_tensor("x_sb", [128, 32 * T], BF16))
        w_sb = en(nc.sbuf_tensor("w3_sb", [128, 2 * 32 * 128], BF16))
        aw_sb = en(nc.sbuf_tensor("aw_sb", [128, 32 * 128], BF16))
        sw_sb = en(nc.sbuf_tensor("sw_sb", [128, 2 * 32 * 128], BF16))
        sp_sb = en(nc.sbuf_tensor("sp_sb", [128, 32], F32))
        ones2_sb = en(nc.sbuf_tensor("ones2_sb", [1, 2], F32))
        onescol = en(nc.sbuf_tensor("onescol", [128, 1], BF16))
        rrow = en(nc.sbuf_tensor("rrow", [1, 128], F32))
        rcol = en(nc.sbuf_tensor("rcol", [128, 1], F32))
        out_t = en(nc.sbuf_tensor("out_t", [128, 2 * T], F32))
        PSm = en(nc.psum_tensor("PSm", [128, T], F32))
        PSr = en(nc.psum_tensor("PSr", [1, 128], F32))
        PSc = en(nc.psum_tensor("PSc", [128, 2], F32))
        SEM = {}
        for s_name in ["GI", "W0", "W1", "PM", "CHD", "ACA", "STO0", "STO1", "SL"]:
            SEM[s_name] = en(nc.semaphore(s_name))
        block = en(nc.Block())
        Wb = [SEM["W0"], SEM["W1"]]
        STp = [SEM["STO0"], SEM["STO1"]]

        def W(eng, s, val):
            if state["emit"]:
                eng.wait_ge(SEM[s] if isinstance(s, str) else s, val)

        def Wm(eng, key):
            if state["emit"]:
                s_name, val = mark[key]
                eng.wait_ge(SEM[s_name], val)

        def inc(instr, s_name, n=1):
            cnt[s_name] = cnt.get(s_name, 0) + n
            if state["emit"]:
                instr.then_inc(SEM[s_name], n)
            return cnt[s_name]

        def setmark(key, s_name):
            mark[key] = (s_name, cnt.get(s_name, 0))

        def act_body(s):
            for j in range(32):
                buf = j % 2
                W(s, Wb[buf], 16 * (j // 2 + 1))
                if j >= 1:
                    Wm(s, f"rmm_{j - 1}")
                if j >= 2:
                    Wm(s, f"smm_{j - 2}")
                s.activation(out=aw_sb[:, :], in_=w_sb[:, buf * 4096:(buf + 1) * 4096],
                             func=AF.Abs, bias=0.0, scale=1.0)
                setmark(f"aw_{j}", "ACA")
                s.activation(out=sw_sb[:, buf * 4096:(buf + 1) * 4096],
                             in_=w_sb[:, buf * 4096:(buf + 1) * 4096],
                             func=AF.Sign, bias=0.0, scale=1.0)
                setmark(f"sw_{j}", "ACA")

        def pe_body(t):
            W(t, "GI", 16)
            W(t, "SL", 16 * 3)
            for j in range(32):
                buf = j % 2
                Wm(t, f"aw_{j}")
                if j >= 1:
                    Wm(t, f"rsdone_{j - 1}")
                i = None
                for a in range(32):
                    i = t.matmul(PSr[0:1, j * 0:128] if False else PSr[0:1, 0:128],
                                 onescol[:, 0:1], aw_sb[:, a * 128:(a + 1) * 128],
                                 start=(a == 0), stop=(a == 31))
                inc(i, "PM")
                mark[f"rmm_{j}"] = ("PM", cnt["PM"])
                Wm(t, f"rrow_{j}")
                inc(t.matmul(PSc[:, 0:2], rrow[0:1, 0:128], ones2_sb[0:1, 0:2],
                             start=True, stop=True), "PM")
                mark[f"cbc_{j}"] = ("PM", cnt["PM"])
                Wm(t, f"sw_{j}")
                if j >= 1:
                    Wm(t, f"oev_{j - 1}")
                i = None
                for tt in range(2):
                    for a in range(32):
                        i = t.matmul(PSm[:, tt * 512:(tt + 1) * 512],
                                     sw_sb[:, buf * 4096 + a * 128:buf * 4096 + (a + 1) * 128],
                                     x_sb[:, a * T + tt * 512:a * T + (tt + 1) * 512],
                                     start=(a == 0), stop=(a == 31))
                inc(i, "PM")
                mark[f"smm_{j}"] = ("PM", cnt["PM"])

        def dve_body(v):
            W(v, "SL", 16 * 3)
            for j in range(32):
                Wm(v, f"rmm_{j}")
                v.tensor_scalar(out=rrow[0:1, :], in0=PSr[0:1, :], scalar1=1.0 / 4096.0,
                                scalar2=1e-8, op0=ALU.mult, op1=ALU.max)
                setmark(f"rrow_{j}", "CHD")
                setmark(f"rsdone_{j}", "CHD")
                Wm(v, f"cbc_{j}")
                v.tensor_copy(rcol[:, 0:1], PSc[:, 0:1])
                v.tensor_tensor(out=rcol[:, 0:1], in0=rcol[:, 0:1], in1=sp_sb[:, j:j + 1],
                                op=ALU.mult)
                setmark(f"rcol_{j}", "CHD")
                Wm(v, f"smm_{j}")
                if j >= 2:
                    W(v, STp[j % 2], 16 * (j // 2))
                v.tensor_scalar(out=out_t[:, (j % 2) * T:(j % 2 + 1) * T], in0=PSm[:, :],
                                scalar1=rcol[:, 0:1], scalar2=None, op0=ALU.mult)
                setmark(f"oev_{j}", "CHD")

        dry = _Dry()
        for _ in range(2):
            cnt.clear()
            pe_body(dry)
            act_body(Chain(dry, None, cnt, "ACA", state))
            dve_body(Chain(dry, None, cnt, "CHD", state))
        cnt.clear()
        state["emit"] = True

        @block.sync
        def _(sync):
            sync.dma_start(out=sp_sb[:, :], in_=sp[:, :]).then_inc(SEM["SL"], 16)
            sync.dma_start(out=ones2_sb[:, :], in_=ones2[:, :]).then_inc(SEM["SL"], 16)
            sync.nop().then_inc(SEM["SL"], 16)
            for j in range(32):
                Wm(sync, f"oev_{j}")
                sync.dma_start(out=bo_o[j * 128:(j + 1) * 128, :],
                               in_=out_t[:, (j % 2) * T:(j % 2 + 1) * T]).then_inc(STp[j % 2], 16)
            sync.wait_ge(SEM["STO0"], 16 * 16)
            sync.wait_ge(SEM["STO1"], 16 * 16)

        @block.gpsimd
        def _(g):
            g.memset(onescol[:, :], 1.0)
            g.dma_start(out=x_sb[:, :].rearrange("p (a t) -> p a t", a=32),
                        in_=x3b.rearrange("(a p) t -> p a t", p=128)).then_inc(SEM["GI"], 16)
            for j in range(32):
                buf = j % 2
                if j >= 2:
                    Wm(g, f"sw_{j - 2}")
                g.dma_start(out=w_sb[:, buf * 4096:(buf + 1) * 4096].rearrange("p (a m) -> p a m", a=32),
                            in_=wTT.rearrange("(a p) o -> p a o", p=128)[:, :, j * 128:(j + 1) * 128])\
                    .then_inc(Wb[buf], 16)

        @block.tensor
        def _(t):
            cnt.clear()
            pe_body(t)

        @block.scalar
        def _(s):
            cnt["ACA"] = 0
            act_body(Chain(s, SEM["ACA"], cnt, "ACA", state))

        @block.vector
        def _(v):
            cnt["CHD"] = 0
            dve_body(Chain(v, SEM["CHD"], cnt, "CHD", state))

    return nc


def headmix_consts():
    J4 = np.zeros((4, 128), np.float32)
    BS4 = np.zeros((128, 4), np.float32)
    for j in range(4):
        J4[j, 32 * j:32 * (j + 1)] = 1.0
        BS4[32 * j:32 * (j + 1), j] = 1.0
    onesrow = np.ones((1, 128), np.float32)
    return dict(J4=J4, BS4=BS4, onesrowh=onesrow)


def build_headmix():
    nc = bass.Bass("TRN2", target_bir_lowering=False)
    q_i = nc.dram_tensor("q", [D, T], BF16, kind="ExternalInput").ap()
    k_i = nc.dram_tensor("k", [D, T], BF16, kind="ExternalInput").ap()
    v_i = nc.dram_tensor("v", [T, D], BF16, kind="ExternalInput").ap()
    mqmk = nc.dram_tensor("mqmk", [1, 64], F32, kind="ExternalInput").ap()
    CONST = {k2: nc.dram_tensor(k2, list(v2.shape), F32, kind="ExternalInput").ap()
             for k2, v2 in headmix_consts().items()}
    c_o = nc.dram_tensor("c_tm", [T, D], BF16, kind="ExternalOutput").ap()

    cnt = {}
    mark = {}
    state = {"emit": False}
    HT = T // 2
    NG = HT // 16
    NGT = 2 * NG
    SCALE = 1.0 / np.sqrt(128.0)

    with ExitStack() as _st:
        en = _st.enter_context
        q_sb = en(nc.sbuf_tensor("q_sb", [128, 32 * HT], BF16))
        k_sb = en(nc.sbuf_tensor("k_sb", [128, 32 * HT], BF16))
        v_sb = en(nc.sbuf_tensor("v_sb", [128, (HT // 4) * 128], BF16))
        eP = en(nc.sbuf_tensor("eP", [128, 2 * 128], BF16))
        rd_sb = en(nc.sbuf_tensor("rd_sb", [4, 2 * 128], F32))
        Pn = en(nc.sbuf_tensor("Pn", [128, 2 * 128], BF16))
        cst = en(nc.sbuf_tensor("cst", [128, 2 * 8 * 512], BF16))
        mq_sb = en(nc.sbuf_tensor("mq_sb", [1, 64], F32))
        negc = en(nc.sbuf_tensor("negc", [128, 1], F32))
        J4_sb = en(nc.sbuf_tensor("J4_sb", [4, 128], F32))
        BS_sb = en(nc.sbuf_tensor("BS_sb", [128, 4], BF16))
        or_sb = en(nc.sbuf_tensor("or_sb", [1, 128], F32))
        PSs0 = en(nc.psum_tensor("PSs0", [128, 128], F32))
        PSs1 = en(nc.psum_tensor("PSs1", [128, 128], F32))
        PSd = en(nc.psum_tensor("PSd", [4, 128], F32))
        PSb = en(nc.psum_tensor("PSb", [128, 128], F32))
        PSc0 = en(nc.psum_tensor("PSc0", [128, 512], F32))
        PSc1 = en(nc.psum_tensor("PSc1", [128, 512], F32))
        PSx = en(nc.psum_tensor("PSx", [128, 2], F32))
        SEM = {}
        for s_name in ["SL", "GB", "QK0", "QK1", "PM", "CHD", "ACA", "STO0", "STO1"]:
            SEM[s_name] = en(nc.semaphore(s_name))
        block = en(nc.Block())
        QKp = [SEM["QK0"], SEM["QK1"]]
        STp = [SEM["STO0"], SEM["STO1"]]
        PSs = [PSs0, PSs1]
        PSc = [PSc0, PSc1]

        def W(eng, s, val):
            if state["emit"]:
                eng.wait_ge(SEM[s] if isinstance(s, str) else s, val)

        def Wm(eng, key):
            if state["emit"]:
                s_name, val = mark[key]
                eng.wait_ge(SEM[s_name], val)

        def inc(instr, s_name, n=1):
            cnt[s_name] = cnt.get(s_name, 0) + n
            if state["emit"]:
                instr.then_inc(SEM[s_name], n)
            return cnt[s_name]

        def setmark(key, s_name):
            mark[key] = (s_name, cnt.get(s_name, 0))

        def hsl(tile, tl):
            return bass.AP(tile.ap().tensor, tl, [[32 * HT, 128], [HT, 32]])

        def pe_body(t):
            W(t, "SL", 32)
            W(t, "GB", 16 * 2)
            Wm(t, "negc_s")
            inc(t.matmul(PSx[:, 0:2], or_sb[0:1, :], mq_sb[0:1, 2:4], start=True, stop=True),
                "PM")
            mark["ncb"] = ("PM", cnt["PM"])
            for hp in range(2):
                W(t, QKp[hp], 16 * 3)
                for g in range(NG):
                    gi = hp * NG + g
                    par = gi % 2
                    if gi >= 2:
                        Wm(t, f"exp_{gi - 2}")
                    i = None
                    for qd in range(4):
                        for j in range(4):
                            tl = g * 16 + qd * 4 + j
                            i = t.matmul(PSs[par][32 * j:32 * (j + 1), qd * 32:(qd + 1) * 32],
                                         hsl(k_sb, tl), hsl(q_sb, tl),
                                         start=True, stop=True, tile_position=(0, 32 * j))
                    inc(i, "PM")
                    mark[f"st_{gi}"] = ("PM", cnt["PM"])
                    Wm(t, f"exp_{gi}")
                    if gi >= 1:
                        Wm(t, f"rd_{gi - 1}")
                    inc(t.matmul(PSd[0:4, :], BS_sb[:, :], eP[:, par * 128:(par + 1) * 128],
                                 start=True, stop=True), "PM")
                    mark[f"dn_{gi}"] = ("PM", cnt["PM"])
                    Wm(t, f"rd_{gi}")
                    if gi >= 1:
                        Wm(t, f"pn_{gi - 1}")
                    inc(t.matmul(PSb[:, :], J4_sb[:, :], rd_sb[0:4, par * 128:(par + 1) * 128],
                                 start=True, stop=True), "PM")
                    mark[f"bc_{gi}"] = ("PM", cnt["PM"])
                    Wm(t, f"pn_{gi}")
                    if gi >= 2:
                        Wm(t, f"cev_{gi - 2}")
                    i = None
                    for qd in range(4):
                        for j in range(4):
                            i = t.matmul(PSc[par][32 * j:32 * (j + 1), qd * 128:(qd + 1) * 128],
                                         Pn[32 * j:32 * (j + 1),
                                            par * 128 + qd * 32:par * 128 + (qd + 1) * 32],
                                         v_sb[32 * j:32 * (j + 1),
                                              (g * 4 + qd) * 128:(g * 4 + qd + 1) * 128],
                                         start=True, stop=True, tile_position=(32 * j, 32 * j))
                    inc(i, "PM")
                    mark[f"pv_{gi}"] = ("PM", cnt["PM"])

        def act_body(s):
            Wm(s, "m2_done")
            s.activation(out=mq_sb[0:1, 2:3], in_=mq_sb[0:1, 2:3], func=AF.Ln, bias=0.0, scale=1.0)
            s.activation(out=mq_sb[0:1, 2:3], in_=mq_sb[0:1, 2:3], func=AF.Exp, bias=0.0, scale=0.5)
            setmark("sqrt_done", "ACA")
            for gi in range(NGT):
                par = gi % 2
                if gi >= 1:
                    gj = gi - 1
                    sbj = gj // 8
                    Wm(s, f"pv_{gj}")
                    if sbj >= 2 and gj % 8 == 0:
                        W(s, STp[sbj % 2], 64 * (sbj // 2))
                    s.activation(out=cst[:, (sbj % 2) * 4096 + (gj % 8) * 512:
                                         (sbj % 2) * 4096 + (gj % 8 + 1) * 512],
                                 in_=PSc[gj % 2][:, :], func=AF.Copy, bias=0.0, scale=1.0)
                    setmark(f"cev_{gj}", "ACA")
                Wm(s, f"st_{gi}")
                if gi >= 2:
                    Wm(s, f"pn_{gi - 2}")
                if gi == 0:
                    Wm(s, "negc_done")
                s.activation(out=eP[:, par * 128:(par + 1) * 128], in_=PSs[par][:, :],
                             func=AF.Exp, bias=negc[:, 0:1], scale=SCALE)
                setmark(f"exp_{gi}", "ACA")
            gj = NGT - 1
            Wm(s, f"pv_{gj}")
            s.activation(out=cst[:, ((gj // 8) % 2) * 4096 + (gj % 8) * 512:
                                 ((gj // 8) % 2) * 4096 + (gj % 8 + 1) * 512],
                         in_=PSc[gj % 2][:, :], func=AF.Copy, bias=0.0, scale=1.0)
            setmark(f"cev_{gj}", "ACA")

        def dve_body(v):
            W(v, "SL", 32)
            v.tensor_scalar(out=mq_sb[0:1, 4:36], in0=mq_sb[0:1, 0:32], scalar1=0.0,
                            scalar2=-1e30, op0=ALU.add, op1=ALU.max, accum_out=mq_sb[0:1, 0:1])
            v.tensor_scalar(out=mq_sb[0:1, 4:36], in0=mq_sb[0:1, 32:64], scalar1=0.0,
                            scalar2=-1e30, op0=ALU.add, op1=ALU.max, accum_out=mq_sb[0:1, 1:2])
            v.tensor_tensor(out=mq_sb[0:1, 2:3], in0=mq_sb[0:1, 0:1], in1=mq_sb[0:1, 1:2],
                            op=ALU.mult)
            v.tensor_copy(mq_sb[0:1, 3:4], mq_sb[0:1, 2:3])
            setmark("m2_done", "CHD")
            Wm(v, "sqrt_done")
            v.tensor_scalar(out=mq_sb[0:1, 2:4], in0=mq_sb[0:1, 2:4], scalar1=-SCALE,
                            scalar2=None, op0=ALU.mult)
            setmark("negc_s", "CHD")
            Wm(v, "ncb")
            v.tensor_copy(negc[:, 0:1], PSx[:, 0:1])
            setmark("negc_done", "CHD")
            for gi in range(NGT):
                par = gi % 2
                Wm(v, f"dn_{gi}")
                v.reciprocal(rd_sb[0:4, par * 128:(par + 1) * 128], PSd[0:4, :])
                setmark(f"rd_{gi}", "CHD")
                Wm(v, f"bc_{gi}")
                v.tensor_tensor(out=Pn[:, par * 128:(par + 1) * 128],
                                in0=eP[:, par * 128:(par + 1) * 128], in1=PSb[:, :], op=ALU.mult)
                setmark(f"pn_{gi}", "CHD")

        dry = _Dry()
        for _ in range(2):
            cnt.clear()
            pe_body(dry)
            act_body(Chain(dry, None, cnt, "ACA", state))
            dve_body(Chain(dry, None, cnt, "CHD", state))
        cnt.clear()
        state["emit"] = True

        @block.sync
        def _(sync):
            sync.dma_start(out=mq_sb[:, :], in_=mqmk[:, :]).then_inc(SEM["SL"], 16)
            sync.dma_start(out=or_sb[:, :], in_=CONST["onesrowh"][:, :]).then_inc(SEM["SL"], 16)
            NSB = NGT // 8

            def store_sb(sb):
                Wm(sync, f"cev_{8 * sb + 7}")
                for j in range(4):
                    src = cst[32 * j:32 * (j + 1), (sb % 2) * 4096:(sb % 2 + 1) * 4096]\
                        .rearrange("p (g qd d) -> p g qd d", g=8, qd=4)
                    dst = bass.AP(c_o.tensor, (sb * 128 + j) * D,
                                  [[128, 32], [16 * D, 8], [4 * D, 4], [1, 128]])
                    sync.dma_start(out=dst, in_=src).then_inc(STp[sb % 2], 16)

            for hp in range(2):
                if hp == 1:
                    for sb in range(NSB // 2):
                        store_sb(sb)
                    Wm(sync, f"pv_{NG - 1}")
                tsl = slice(hp * HT, (hp + 1) * HT)
                sync.dma_start(out=q_sb[:, :].rearrange("p (h t) -> p h t", h=32),
                               in_=q_i.rearrange("(h p) t -> p h t", p=128)[:, :, tsl])\
                    .then_inc(QKp[hp], 16)
                sync.dma_start(out=k_sb[:, :].rearrange("p (h t) -> p h t", h=32),
                               in_=k_i.rearrange("(h p) t -> p h t", p=128)[:, :, tsl])\
                    .then_inc(QKp[hp], 16)
                vsrc = bass.AP(v_i.tensor, hp * HT * D,
                               [[128, 128], [128 * 128, HT // 4], [1, 128]])
                sync.dma_start(out=v_sb[:, :].rearrange("p (tq d) -> p tq d", d=128),
                               in_=vsrc).then_inc(QKp[hp], 16)
            for sb in range(NSB // 2, NSB):
                store_sb(sb)
            sync.wait_ge(SEM["STO0"], 64 * (NSB // 2))
            sync.wait_ge(SEM["STO1"], 64 * (NSB // 2))

        @block.gpsimd
        def _(g):
            g.dma_start(out=J4_sb[:, :], in_=CONST["J4"][:, :]).then_inc(SEM["GB"], 16)
            g.dma_start(out=BS_sb[:, :], in_=CONST["BS4"][:, :]).then_inc(SEM["GB"], 16)

        @block.tensor
        def _(t):
            cnt.clear()
            pe_body(t)

        @block.scalar
        def _(s):
            cnt["ACA"] = 0
            act_body(Chain(s, SEM["ACA"], cnt, "ACA", state))

        @block.vector
        def _(v):
            cnt["CHD"] = 0
            dve_body(Chain(v, SEM["CHD"], cnt, "CHD", state))

    return nc


from concourse.bass_utils import run_bass_kernel_spmd

_PROG_CACHE = {}


def _prog(name, fn):
    if name not in _PROG_CACHE:
        _PROG_CACHE[name] = fn()
    return _PROG_CACHE[name]


def _asc(a):
    return np.ascontiguousarray(a)


def host_col(b):
    return np.ascontiguousarray(np.asarray(b).reshape(-1, 128).T)


def kernel(**inp):
    import ml_dtypes
    BF = ml_dtypes.bfloat16
    cores = list(range(8))

    x0 = np.asarray(inp["L0_attn_hidden_states"], np.float32).reshape(8192, 4096)
    x0t = _asc(x0.T)

    nc = _prog("qkv", build_qkv)
    wqT = _asc(inp["L0_attn_wq"].T); wkT = _asc(inp["L0_attn_wk"].T)
    wvT = _asc(inp["L0_attn_wv"].T)
    bq_c = host_col(inp["L0_attn_bq"]); bk_c = host_col(inp["L0_attn_bk"])
    bv_row = _asc(inp["L0_attn_bv"].reshape(1, -1))
    im = []
    for c in cores:
        sl = slice(1024 * c, 1024 * (c + 1))
        im.append({"x0t": _asc(x0t[:, sl]), "wqT": wqT, "wkT": wkT, "wvT": wvT,
                   "bq_c": bq_c, "bk_c": bk_c, "bv_row": bv_row})
    r1 = run_bass_kernel_spmd(nc, im, core_ids=cores).results

    nc = _prog("headmix", build_headmix)
    hm_c = headmix_consts()
    im = []
    for c in cores:
        m = {"q": np.asarray(r1[c]["q"]), "k": np.asarray(r1[c]["k"]),
             "v": np.asarray(r1[c]["v"]), "mqmk": np.asarray(r1[c]["mqmk"])}
        for k2, v2 in hm_c.items():
            m[k2] = v2
        im.append(m)
    r2 = run_bass_kernel_spmd(nc, im, core_ids=cores).results

    nc = _prog("addln_tm", lambda: build_addln(True, True))
    im = []
    for c in cores:
        sl = slice(1024 * c, 1024 * (c + 1))
        im.append({"at": _asc(x0t[:, sl]), "bt": np.asarray(r2[c]["c_tm"])})
    r3 = run_bass_kernel_spmd(nc, im, core_ids=cores).results

    nc = _prog("ffn", build_ffn)
    W_res, W_pre, W_post = inp["L0_ffn_W_res"], inp["L0_ffn_W_pre"], inp["L0_ffn_W_post"]
    wcat = np.zeros((96, 4096), np.float32)
    wcat[0:4] = W_pre; wcat[32:36] = W_post; wcat[64:80] = W_res
    basec = np.zeros((96, 1), np.float32)
    basec[0:4, 0] = inp["L0_ffn_H_pre_base"]
    basec[32:36, 0] = inp["L0_ffn_H_post_base"]
    basec[64:80, 0] = inp["L0_ffn_H_res_base"].reshape(16)
    ffn_m = {"wcatT": _asc(wcat.T), "basec": basec,
             "gate": np.asarray(inp["L0_ffn_gate"], np.float32).reshape(1, 1),
             "wbranch": _asc(inp["L0_ffn_W_branch"])}
    for k, v in ffn_consts().items():
        ffn_m[k] = v
    im = []
    for c in cores:
        m = dict(ffn_m)
        m["x1f"] = np.asarray(r3[c]["xf"])
        m["x1b"] = np.asarray(r3[c]["xb"])
        im.append(m)
    r4 = run_bass_kernel_spmd(nc, im, core_ids=cores).results

    nc = _prog("addln_f", lambda: build_addln(False))
    zeros_f = np.zeros((4096, 1024), np.float32)
    im = [{"at": np.asarray(r4[c]["xr2"]), "bt": zeros_f} for c in cores]
    r5 = run_bass_kernel_spmd(nc, im, core_ids=cores).results

    nc = _prog("l1proj", build_l1proj)
    x2b_full = np.concatenate([np.asarray(r5[c]["xb"]) for c in cores], axis=1)
    cw = inp["L1_attn_conv_w"].reshape(80, 128, 4).transpose(1, 0, 2).reshape(128, 320)
    l1_m = {"inprojT": _asc(inp["L1_attn_in_proj_w"].T), "convw_c": _asc(cw),
            "convb_c": _asc(inp["L1_attn_conv_b"].reshape(80, 128).T),
            "dtbias_col": _asc(inp["L1_attn_dt_bias"].reshape(128, 1)),
            "alog_col": _asc(inp["L1_attn_A_log"].reshape(128, 1))}
    im = []
    for c in cores:
        s0 = 1024 * c
        if c % 4 == 0:
            halo = np.zeros((4096, 3), x2b_full.dtype)
        else:
            halo = x2b_full[:, s0 - 3:s0]
        m = dict(l1_m)
        m["x2b"] = _asc(np.concatenate([halo, x2b_full[:, s0:s0 + 1024]], axis=1))
        im.append(m)
    r6 = run_bass_kernel_spmd(nc, im, core_ids=cores).results

    nc = _prog("ssd", build_ssd)
    hs_full = np.concatenate([np.asarray(r6[c]["hs"]) for c in cores], axis=1)
    bc_full = np.concatenate([np.asarray(r6[c]["bc"]) for c in cores], axis=1)
    dt_full = np.concatenate([np.asarray(r6[c]["dt"]) for c in cores], axis=1)
    dA_full = np.concatenate([np.asarray(r6[c]["dA"]) for c in cores], axis=1)
    ssd_c = ssd_consts()
    im = []
    for g in cores:
        Dv = inp["L1_attn_D"][16 * g:16 * (g + 1)]
        dcols = np.zeros((128, 8), np.float32)
        for r in range(128):
            for a in range(8):
                dcols[r, a] = Dv[2 * a + r // 64]
        m = {"hsg": _asc(hs_full[1024 * g:1024 * (g + 1)]),
             "Bg": _asc(bc_full[128 * g:128 * (g + 1)]),
             "Cg": _asc(bc_full[1024 + 128 * g:1024 + 128 * (g + 1)]),
             "dtg": _asc(dt_full[16 * g:16 * (g + 1)]),
             "dAg": _asc(dA_full[16 * g:16 * (g + 1)]), "dcols": dcols}
        for k, v in ssd_c.items():
            m[k] = v
        im.append(m)
    r7 = run_bass_kernel_spmd(nc, im, core_ids=cores).results

    nc = _prog("gateout", build_gateout)
    y_full = np.concatenate([np.asarray(r7[g]["y"]) for g in cores], axis=0)
    go_m = {"outwT": _asc(inp["L1_attn_out_w"].T),
            "normw_c": _asc(inp["L1_attn_norm_w"].reshape(64, 128).T),
            "onesrow": np.ones((1, 128), np.float32),
            "eps1": np.full((1, 1), 1e-5, np.float32)}
    im = []
    for c in cores:
        m = dict(go_m)
        m["yg"] = _asc(y_full[:, 1024 * c:1024 * (c + 1)])
        m["z"] = np.asarray(r6[c]["z"])
        im.append(m)
    r8 = run_bass_kernel_spmd(nc, im, core_ids=cores).results

    nc = _prog("addln_b", lambda: build_addln(True))
    im = [{"at": np.asarray(r5[c]["xf"]), "bt": np.asarray(r8[c]["ao"])} for c in cores]
    r9 = run_bass_kernel_spmd(nc, im, core_ids=cores).results

    nc = _prog("binffn", build_binffn)
    bf_m = {"wTT": _asc(inp["L1_ffn_weight"].T),
            "sp_c": _asc(inp["L1_ffn_scale_param"].reshape(32, 128).T),
            "ones2": np.ones((1, 2), np.float32)}
    im = []
    for c in cores:
        m = dict(bf_m)
        m["x3b"] = np.asarray(r9[c]["xb"])
        im.append(m)
    r10 = run_bass_kernel_spmd(nc, im, core_ids=cores).results

    nc = _prog("addln_f", lambda: build_addln(False))
    im = [{"at": np.asarray(r9[c]["xf"]), "bt": np.asarray(r10[c]["bo"])} for c in cores]
    r11 = run_bass_kernel_spmd(nc, im, core_ids=cores).results

    outT = np.concatenate([np.asarray(r11[c]["xf"], np.float32) for c in cores], axis=1)
    return _asc(outT.T).reshape(2, 4096, 4096)
